# version 12
# baseline (speedup 1.0000x reference)
import numpy as np
from contextlib import ExitStack

import concourse.bass as bass
import concourse.mybir as mybir
from concourse.bass_utils import run_bass_kernel_spmd

F32 = mybir.dt.float32
BF16 = mybir.dt.bfloat16
AF = mybir.ActivationFunctionType
ALU = mybir.AluOpType
AX = mybir.AxisListType

D = 2048
S = 2048
TBLK = 512
NT = TBLK // 128
NBLK = S // TBLK
NE = 8
CH = 16
LN_EPS = 1e-5
AT_EPS = 1e-6
BIG = 1e30
SERIAL = False
SER_SECTIONS = ""
ATTN_SUB = 7

ENGINES = ("sync", "gpsimd", "vector", "scalar", "tensor")


class Sem:
    def __init__(self, ctx, nc, name):
        self.h = ctx.enter_context(nc.semaphore(name))
        self.n = 0


class Plan:
    def __init__(self, block, sems):
        self.block = block
        self.sems = sems
        self.ops = {e: [] for e in ENGINES}
        self.force_serial = False
        self.last_inc = {}

    def op(self, eng, fn, waits=(), inc=None, k=1):
        if SERIAL or self.force_serial:
            waits = [(s, s.n) for s in self.sems]
        else:
            waits = list(waits)
            auto = self.last_inc.get((eng, k == 16))
            if auto is not None:
                waits.append(auto)
        if inc is not None:
            inc.n += k
            v = inc.n
            self.last_inc[(eng, k == 16)] = (inc, v)
        else:
            v = None
        best = {}
        for s, w in waits:
            if w > 0 and (s not in best or w > best[s]):
                best[s] = w
        ws = tuple(best.items())
        self.ops[eng].append((fn, ws, inc, k))
        return v

    def barrier(self):
        snap = [(s, s.n) for s in self.sems]
        for e in ENGINES:
            self.ops[e].append((None, tuple((s, w) for s, w in snap if w > 0), None, 0))

    def flush(self):
        for ename in ENGINES:
            lst = self.ops[ename]
            if not lst:
                continue

            def body(eng, lst=lst):
                for fn, waits, inc, k in lst:
                    for sem, v in waits:
                        eng.wait_ge(sem.h, v)
                    ins = fn(eng) if fn is not None else None
                    if inc is not None:
                        (ins if ins is not None else eng.nop()).then_inc(inc.h, k)

            getattr(self.block, ename)(body)
        self.ops = {e: [] for e in ENGINES}


def build(dump=(), phases=3):
    nc = bass.Bass("TRN2", target_bir_lowering=False)

    hs = nc.dram_tensor("hs", [S, D], F32, kind="ExternalInput").ap()
    wfs = nc.dram_tensor("wfs", [NE, 16, 2, 128, D], F32, kind="ExternalInput").ap()
    w2s = nc.dram_tensor("w2s", [NE, 16, 128, D], F32, kind="ExternalInput").ap()
    wq_r = nc.dram_tensor("wq_r", [16, 128, D], F32, kind="ExternalInput").ap()
    wk_r = nc.dram_tensor("wk_r", [16, 128, D], F32, kind="ExternalInput").ap()
    wvT_d = nc.dram_tensor("wvT_d", [16, 128, D], F32, kind="ExternalInput").ap()
    woT_d = nc.dram_tensor("woT_d", [16, 128, D], F32, kind="ExternalInput").ap()
    wmq_d = nc.dram_tensor("wmq_d", [128, 64], F32, kind="ExternalInput").ap()
    wmk_d = nc.dram_tensor("wmk_d", [128, 64], F32, kind="ExternalInput").ap()
    bqT_d = nc.dram_tensor("bqT_d", [128, 16], F32, kind="ExternalInput").ap()
    bkT_d = nc.dram_tensor("bkT_d", [128, 16], F32, kind="ExternalInput").ap()
    bv_d = nc.dram_tensor("bv_d", [1, D], F32, kind="ExternalInput").ap()
    bo_d = nc.dram_tensor("bo_d", [1, D], F32, kind="ExternalInput").ap()
    gw_d = nc.dram_tensor("gw_d", [128, 112], F32, kind="ExternalInput").ap()
    gb_d = nc.dram_tensor("gb_d", [1, 7], F32, kind="ExternalInput").ap()
    be_d = nc.dram_tensor("be_d", [1, 7], F32, kind="ExternalInput").ap()
    idf_d = nc.dram_tensor("idf_d", [128, 128], F32, kind="ExternalInput").ap()
    idb_d = nc.dram_tensor("idb_d", [128, 128], BF16, kind="ExternalInput").ap()

    out_d = nc.dram_tensor("out", [S, D], F32, kind="ExternalOutput").ap()
    xA = nc.dram_tensor("xA", [S, D], F32,
                        kind="ExternalOutput" if "xA" in dump else "Internal").ap()
    xB = nc.dram_tensor("xB", [S, D], F32,
                        kind="ExternalOutput" if "xB" in dump else "Internal").ap()
    cwf = nc.dram_tensor("cwf", [NE, 16, 2, 128, D], BF16, kind="Internal").ap()
    cw2 = nc.dram_tensor("cw2", [NE, 16, 128, D], BF16, kind="Internal").ap()
    cwq = nc.dram_tensor("cwq", [16, 128, D], BF16, kind="Internal").ap()
    cwk = nc.dram_tensor("cwk", [16, 128, D], BF16, kind="Internal").ap()
    cwvT = nc.dram_tensor("cwvT", [16, 128, D], BF16, kind="Internal").ap()
    cwoT = nc.dram_tensor("cwoT", [16, 128, D], BF16, kind="Internal").ap()

    with ExitStack() as ctx:
        block = ctx.enter_context(nc.Block())
        sW = Sem(ctx, nc, "sW")
        sG = Sem(ctx, nc, "sG")
        sX = Sem(ctx, nc, "sX")
        sPE = Sem(ctx, nc, "sPE")
        sDV = Sem(ctx, nc, "sDV")
        sAC = Sem(ctx, nc, "sAC")
        allsems = (sW, sG, sX, sPE, sDV, sAC)

        ident_f = ctx.enter_context(nc.sbuf_tensor("ident_f", [128, 128], F32))
        ident_b = ctx.enter_context(nc.sbuf_tensor("ident_b", [128, 128], BF16))
        gw_sb = ctx.enter_context(nc.sbuf_tensor("gw_sb", [128, 112], F32))
        gb_sb = ctx.enter_context(nc.sbuf_tensor("gb_sb", [1, 7], F32))
        be_bc = ctx.enter_context(nc.sbuf_tensor("be_bc", [128, 7], F32))
        ones_f = ctx.enter_context(nc.sbuf_tensor("ones_f", [1, 128], F32))
        ones_b = ctx.enter_context(nc.sbuf_tensor("ones_b", [1, 128], BF16))
        onescol_b = ctx.enter_context(nc.sbuf_tensor("onescol_b", [128, 1], BF16))
        st = ctx.enter_context(nc.sbuf_tensor("st", [128, 16], F32))
        g_e = ctx.enter_context(nc.sbuf_tensor("g_e", [128, 7], F32))
        g_p = ctx.enter_context(nc.sbuf_tensor("g_p", [128, 7], F32))
        g_b = ctx.enter_context(nc.sbuf_tensor("g_b", [128, 7], F32))
        g_b2 = ctx.enter_context(nc.sbuf_tensor("g_b2", [128, 7], F32))
        g_t1 = ctx.enter_context(nc.sbuf_tensor("g_t1", [128, 7], F32))
        g_t2 = ctx.enter_context(nc.sbuf_tensor("g_t2", [128, 7], F32))
        g_s = ctx.enter_context(nc.sbuf_tensor("g_s", [128, 8], F32))
        w_all = ctx.enter_context(nc.sbuf_tensor("w_all", [128, NT * NE], F32))

        ps_a = ctx.enter_context(nc.psum_tensor("ps_a", [128, 1024], F32))
        ps_b = ctx.enter_context(nc.psum_tensor("ps_b", [128, 1024], F32))
        ps_c = ctx.enter_context(nc.psum_tensor("ps_c", [128, 2048], F32))

        plan = Plan(block, allsems)
        P = plan.op

        P("sync", lambda e: e.dma_start(out=ident_f[:], in_=idf_d), inc=sW, k=16)
        P("sync", lambda e: e.dma_start(out=ident_b[:], in_=idb_d), inc=sW, k=16)
        P("sync", lambda e: e.dma_start(out=gw_sb[:], in_=gw_d), inc=sW, k=16)
        P("sync", lambda e: e.dma_start(out=gb_sb[:], in_=gb_d), inc=sW, k=16)
        vbe = P("sync", lambda e: e.dma_start(out=g_e[0:1, :], in_=be_d), inc=sW, k=16)
        P("vector", lambda e: e.memset(ones_f[:], 1.0), inc=sDV)
        P("vector", lambda e: e.memset(ones_b[:], 1.0), inc=sDV)
        vms = P("vector", lambda e: e.memset(onescol_b[:], 1.0), inc=sDV)
        vpe = P("tensor", lambda e: e.matmul(ps_a[:, 0:7], ones_f[:], g_e[0:1, :],
                                             start=True, stop=True),
                waits=[(sW, vbe), (sDV, vms)], inc=sPE)
        P("vector", lambda e: e.tensor_copy(be_bc[:], ps_a[:, 0:7]),
          waits=[(sPE, vpe)], inc=sDV)
        plan.barrier()

        def emit_ln_stats(xa_ap, base, wx, sq_out, sq_waits=()):
            vsq = P("scalar", lambda e: e.activation(
                sq_out, xa_ap, AF.Square, accum_out=st[:, base + 1:base + 2]),
                waits=list(wx) + list(sq_waits), inc=sAC)
            P("vector", lambda e: e.tensor_reduce(
                st[:, base:base + 1], xa_ap, axis=AX.X, op=ALU.add), waits=wx, inc=sDV)
            P("vector", lambda e: e.tensor_scalar(
                st[:, base + 2:base + 3], st[:, base:base + 1], 1.0 / D, None,
                op0=ALU.mult), waits=[(sAC, vsq)], inc=sDV)
            P("vector", lambda e: e.tensor_scalar(
                st[:, base + 3:base + 4], st[:, base + 1:base + 2], 1.0 / D, LN_EPS,
                op0=ALU.mult, op1=ALU.add), inc=sDV)
            P("vector", lambda e: e.tensor_tensor(
                st[:, base + 4:base + 5], st[:, base + 2:base + 3],
                st[:, base + 2:base + 3], op=ALU.mult), inc=sDV)
            vvar = P("vector", lambda e: e.tensor_tensor(
                st[:, base + 3:base + 4], st[:, base + 3:base + 4],
                st[:, base + 4:base + 5], op=ALU.subtract), inc=sDV)
            vsd = P("scalar", lambda e: e.activation(
                st[:, base + 4:base + 5], st[:, base + 3:base + 4], AF.Sqrt),
                waits=[(sDV, vvar)], inc=sAC)
            P("vector", lambda e: e.reciprocal(
                st[:, base + 5:base + 6], st[:, base + 4:base + 5]),
                waits=[(sAC, vsd)], inc=sDV)
            vnb = P("vector", lambda e: e.scalar_tensor_tensor(
                st[:, base + 6:base + 7], st[:, base + 2:base + 3], -1.0,
                st[:, base + 5:base + 6], op0=ALU.mult, op1=ALU.mult), inc=sDV)
            return vnb

        def emit_ffn(call, src, dst):
            first = (call == 0)
            with ExitStack() as fs:
                def sb(name, shape, dt):
                    return fs.enter_context(nc.sbuf_tensor(f"{name}_{call}", shape, dt))
                x_blk = sb("x_blk", [128, NT * D], F32)
                x_ln = sb("x_ln", [128, D], F32)
                yT_bufs = [sb("yT_bA", [128, CH * TBLK], BF16),
                           sb("yT_bB", [128, CH * TBLK], BF16)]
                w_alls = [sb("w_allA", [128, NT * NE], F32),
                          sb("w_allB", [128, NT * NE], F32)]
                yT32 = sb("yT32", [128, D], F32)
                y_tm = sb("y_tm", [128, D], F32)
                h_sb = sb("h_sb", [128, 16 * TBLK], BF16)
                w2_sb = sb("w2_sb", [128, CH * D], BF16)
                wf_rg = sb("wf_rg", [128, 2 * 2 * 2 * D], BF16)
                s_rg = sb("s_rg", [128, 2 * TBLK], F32)

                def mmA(e, ybuf, mat, slot, bank):
                    base = slot * 2 * D + mat * D
                    last = None
                    for c in range(CH):
                        last = e.matmul(
                            bank,
                            wf_rg[:, base + c * 128:base + (c + 1) * 128],
                            ybuf[:, c * TBLK:(c + 1) * TBLK],
                            start=(c == 0), stop=(c == CH - 1))
                    return last

                plan.barrier()
                cache_mark = [0] * NE
                w2_cache_mark = 0
                prev_silu = [0, 0]
                prev_mult = [0, 0]
                prev_pairA = [0, 0, 0, 0]
                prev_multw = [0, 0]
                passB_pe_end = 0
                sttB_tail = [0, 0]
                cache_pair_done = [0] * 16
                store_mark = 0
                store_sem = sX
                pe_blk_end = [0, 0]
                M = dict(xln_ac={0: 0, 1: 0}, xln_dv=0,
                         ytm_pe=0,
                         b_free={2: (0, 0), 3: (0, 0)},
                         y32_pe=0,
                         vexp=0)

                def emit_ln_ti_pre(blk, ti):
                    t0 = blk * TBLK + ti * 128
                    base = (ti % 2) * 8
                    vx = P("scalar", lambda e, t0=t0: e.dma_start(
                        out=x_ln[:], in_=src[t0:t0 + 128, :]),
                        waits=[(sAC, M["xln_ac"][0]), (sDV, M["xln_dv"])], inc=sX, k=16)
                    vnb = emit_ln_stats(x_ln[:], base, [(sX, vx)], y_tm[:],
                                        [(sPE, M["ytm_pe"])])
                    wn = [(sDV, vnb)]
                    vy = P("scalar", lambda e, base=base: e.activation(
                        y_tm[:], x_ln[:],
                        AF.Identity, bias=st[:, base + 6:base + 7],
                        scale=st[:, base + 5:base + 6]), waits=wn, inc=sAC)
                    M["xln_ac"][0] = vy
                    M["xln_dv"] = vnb
                    return vy

                def emit_ln_ti(blk, ti, ybuf, wbuf, war_pe, vy=None):
                    if True:
                        base = (ti % 2) * 8
                        sl = ti % 2
                        if vy is None:
                            vy = emit_ln_ti_pre(blk, ti)
                        for g in range(4):
                            bk = 2 + (g % 2)
                            reg = 1024 + (g % 2) * 512
                            wt = [(sAC, vy)]
                            fdv, fac = M["b_free"][bk]
                            if fdv:
                                wt.append((sDV, fdv))
                            if fac:
                                wt.append((sAC, fac))
                            if bk == 2:
                                wt.append((sAC, M["vexp"]))
                            if g == 0 and war_pe:
                                wt.append((sPE, war_pe))
                            if g == 0:
                                wt.append((sPE, M["y32_pe"]))

                            def dot(e, g=g, reg=reg):
                                last = None
                                for i in range(4):
                                    last = e.transpose(
                                        ps_c[:, reg + i * 128:reg + (i + 1) * 128],
                                        y_tm[:, (g * 4 + i) * 128:(g * 4 + i + 1) * 128],
                                        ident_f[:])
                                return last
                            vt = P("tensor", dot, waits=wt, inc=sPE)
                            v1 = P("vector", lambda e, g=g, reg=reg: e.tensor_copy(
                                yT32[:, g * 512:(g + 1) * 512],
                                ps_c[:, reg:reg + 512]), waits=[(sPE, vt)], inc=sDV)
                            vc = P("scalar", lambda e, g=g, reg=reg, ti=ti, ybuf=ybuf: e.activation(
                                ybuf.ap().rearrange("p (c t) -> p c t", c=CH)
                                [:, 4 * g:4 * g + 4, ti * 128:(ti + 1) * 128],
                                ps_c.ap().rearrange("p (c t) -> p c t", c=16)
                                [:, (reg // 128):(reg // 128) + 4, :],
                                AF.Copy), waits=[(sPE, vt), (sDV, v1)] +
                                ([(sPE, war_pe)] if (g == 0 and ti == 0 and war_pe) else []),
                                inc=sAC)
                            M["b_free"][bk] = (v1, vc)
                        M["ytm_pe"] = sPE.n
                        yT32_done = sDV.n

                        fdv, fac = M["b_free"][2]
                        wg_ = [(sDV, yT32_done), (sDV, fdv), (sAC, fac)]

                        def dogate(e):
                            for c in range(CH):
                                e.matmul(ps_c[:, 1024:1031],
                                         yT32[:, c * 128:(c + 1) * 128],
                                         gw_sb[:, c * 7:(c + 1) * 7],
                                         start=(c == 0), stop=False)
                            return e.matmul(ps_c[:, 1024:1031], ones_f[:], gb_sb[:],
                                            start=False, stop=True)
                        vgm = P("tensor", dogate, waits=wg_, inc=sPE)
                        M["y32_pe"] = vgm
                        P("vector", lambda e: e.tensor_reduce(
                            g_s[:, 0:1], ps_c[:, 1024:1031], axis=AX.X, op=ALU.max),
                            waits=[(sPE, vgm)], inc=sDV)
                        vneg = P("vector", lambda e: e.tensor_scalar(
                            g_s[:, 1:2], g_s[:, 0:1], -1.0, None, op0=ALU.mult), inc=sDV)
                        vexp = P("scalar", lambda e: e.activation(
                            g_e[:], ps_c[:, 1024:1031], AF.Exp, bias=g_s[:, 1:2]),
                            waits=[(sDV, vneg)], inc=sAC)
                        M["vexp"] = vexp
                        P("vector", lambda e: e.tensor_reduce(
                            g_s[:, 2:3], g_e[:], axis=AX.X, op=ALU.add),
                            waits=[(sAC, vexp)], inc=sDV)
                        P("vector", lambda e: e.reciprocal(g_s[:, 3:4], g_s[:, 2:3]), inc=sDV)
                        P("vector", lambda e: e.tensor_scalar(
                            g_p[:], g_e[:], g_s[:, 3:4], None, op0=ALU.mult), inc=sDV)
                        P("vector", lambda e: e.tensor_tensor(
                            g_b[:], g_p[:], be_bc[:], op=ALU.add), inc=sDV)
                        P("vector", lambda e: e.tensor_reduce(
                            g_s[:, 4:5], g_b[:], axis=AX.X, op=ALU.max), inc=sDV)
                        P("vector", lambda e: e.tensor_scalar(
                            g_t1[:], g_b[:], g_s[:, 4:5], None, op0=ALU.is_ge), inc=sDV)
                        P("vector", lambda e: e.scalar_tensor_tensor(
                            g_b2[:], g_t1[:], -BIG, g_b[:], op0=ALU.mult, op1=ALU.add), inc=sDV)
                        P("vector", lambda e: e.tensor_reduce(
                            g_s[:, 4:5], g_b2[:], axis=AX.X, op=ALU.max), inc=sDV)
                        P("vector", lambda e: e.tensor_scalar(
                            g_t2[:], g_b2[:], g_s[:, 4:5], None, op0=ALU.is_ge), inc=sDV)
                        P("vector", lambda e: e.scalar_tensor_tensor(
                            g_b2[:], g_t2[:], -BIG, g_b2[:], op0=ALU.mult, op1=ALU.add), inc=sDV)
                        P("vector", lambda e: e.tensor_reduce(
                            g_s[:, 4:5], g_b2[:], axis=AX.X, op=ALU.max), inc=sDV)
                        P("vector", lambda e: e.tensor_scalar(
                            g_b2[:], g_b2[:], g_s[:, 4:5], None, op0=ALU.is_ge), inc=sDV)
                        P("vector", lambda e: e.tensor_tensor(
                            g_t1[:], g_t1[:], g_t2[:], op=ALU.add), inc=sDV)
                        P("vector", lambda e: e.tensor_tensor(
                            g_t1[:], g_t1[:], g_b2[:], op=ALU.add), inc=sDV)
                        P("vector", lambda e: e.tensor_tensor(
                            g_p[:], g_p[:], g_t1[:], op=ALU.mult), inc=sDV)
                        P("vector", lambda e: e.tensor_reduce(
                            g_s[:, 5:6], g_p[:], axis=AX.X, op=ALU.add), inc=sDV)
                        P("vector", lambda e: e.reciprocal(g_s[:, 6:7], g_s[:, 5:6]), inc=sDV)
                        P("vector", lambda e: e.tensor_scalar(
                            g_s[:, 7:8], g_s[:, 6:7], 0.5, None, op0=ALU.mult), inc=sDV)
                        if ti == 0:
                            P("vector", lambda e, wbuf=wbuf: e.memset(wbuf[:], 0.5), inc=sDV)
                        P("vector", lambda e, ti=ti, wbuf=wbuf: e.tensor_scalar(
                            wbuf[:, ti * NE + 1:(ti + 1) * NE], g_p[:], g_s[:, 7:8],
                            None, op0=ALU.mult), inc=sDV)
                    return sAC.n, sDV.n

                def slab_src(eidx, j, use32):
                    t = wfs if use32 else cwf
                    return t[eidx, j].rearrange("m p f -> p m f")

                yT_done = [0, 0]
                for ti0 in range(NT):
                    yT_done[0], _ = emit_ln_ti(0, ti0, yT_bufs[0], w_alls[0], 0)

                for blk in range(NBLK):
                    par = blk % 2
                    ybuf = yT_bufs[par]
                    wbuf = w_alls[par]
                    ln_act_last = yT_done[par]
                    wxb = [] if blk == 0 else [(store_sem, store_mark)]
                    vxblk = P("scalar", lambda e, blk=blk: e.dma_start(
                        out=x_blk.ap().rearrange("p (t f) -> p t f", t=NT),
                        in_=src[blk * TBLK:(blk + 1) * TBLK, :].rearrange(
                            "(t p) f -> p t f", p=128)), waits=wxb, inc=sX, k=16)

                    for ei in range(NE):
                        use32 = first and blk == 0
                        conv = use32
                        qeng = "gpsimd" if use32 else "sync"
                        qsem = sG if use32 else sW
                        w2src = w2s[ei] if use32 else cw2[ei]
                        ww2 = [(sPE, passB_pe_end)]
                        if w2_cache_mark:
                            ww2.append((sW, w2_cache_mark))
                        if not use32 and first:
                            ww2.append((sW, cache_mark[ei]))
                        if not conv:
                            vW2 = P("gpsimd", lambda e, w2src=w2src: e.dma_start(
                                out=w2_sb.ap().rearrange("p (j f) -> p j f", j=16),
                                in_=w2src.rearrange("j p f -> p j f")), waits=ww2, inc=sG, k=16)
                        if first and blk >= 2 and ei in (1, 3):
                            srcT, dstT = {(2, 1): (wq_r, cwq), (2, 3): (wk_r, cwk),
                                          (3, 1): (wvT_d, cwvT), (3, 3): (woT_d, cwoT)}[(blk, ei)]
                            P("gpsimd", lambda e, srcT=srcT, dstT=dstT: e.dma_start(
                                out=dstT, in_=srcT), inc=sG, k=16)

                        pair_done = [0] * 16
                        pairA_end = [0] * 16
                        if not conv:
                            t_ = wfs if use32 else cwf
                            wv_ = [(sPE, max(prev_pairA))]
                            cpd = max(cache_pair_done[12:16])
                            if cpd:
                                wv_.append((sW, cpd))
                            if not use32 and first:
                                wv_.append((sW, cache_mark[ei]))
                            v4 = P(qeng, lambda e, t_=t_, ei=ei: e.dma_start(
                                out=wf_rg.ap().rearrange("p (s m f) -> p s m f", s=4, m=2),
                                in_=t_[ei, 0:4].rearrange("j m p f -> p j m f")),
                                waits=wv_, inc=qsem, k=16)
                            pair_done[0] = pair_done[1] = pair_done[2] = pair_done[3] = v4
                        else:
                            for jp in (0, 2):
                                wv_ = [(sPE, max(prev_pairA))]
                                cpd = max(cache_pair_done[12:16])
                                if cpd:
                                    wv_.append((sW, cpd))
                                v2 = P(qeng, lambda e, ei=ei, jp=jp: e.dma_start(
                                    out=wf_rg.ap().rearrange("p (s m f) -> p s m f", s=4, m=2)[:, jp:jp + 2],
                                    in_=wfs[ei, jp:jp + 2].rearrange("j m p f -> p j m f")),
                                    waits=wv_, inc=qsem, k=16)
                                pair_done[jp] = pair_done[jp + 1] = v2
                            cache_pair_done[0] = cache_pair_done[1] = \
                                cache_pair_done[2] = cache_pair_done[3] = P(
                                "sync", lambda e, ei=ei: e.dma_start(
                                    out=cwf[ei, 0:4].rearrange("j m p f -> p j m f"),
                                    in_=wf_rg.ap().rearrange("p (s m f) -> p s m f", s=4, m=2)),
                                waits=[(sG, pair_done[3])], inc=sW, k=16)

                        if conv:
                            for hf in range(2):
                                vW2 = P("gpsimd", lambda e, w2src=w2src, hf=hf: e.dma_start(
                                    out=w2_sb.ap().rearrange("p (j f) -> p j f", j=16)[:, hf * 8:(hf + 1) * 8],
                                    in_=w2src[hf * 8:(hf + 1) * 8].rearrange("j p f -> p j f")),
                                    waits=ww2 if hf == 0 else [], inc=sG, k=16)
                            w2_cache_mark = P("sync", lambda e, ei=ei: e.dma_start(
                                out=cw2[ei].rearrange("j p f -> p j f"),
                                in_=w2_sb.ap().rearrange("p (j f) -> p j f", j=16)),
                                waits=[(sG, vW2)], inc=sW, k=16)

                        silu_m = [0] * 16
                        mult_m = [0] * 16
                        h1_pe = [0] * 16
                        hg_pe = [0] * 16
                        for j in range(16):
                            slot = j % 4
                            bank_h1 = ps_a[:, (j % 2) * 512:(j % 2) * 512 + 512]
                            bank_hg = ps_b[:, (j % 2) * 512:(j % 2) * 512 + 512]
                            wA = [(sAC, ln_act_last), (qsem, pair_done[j])]
                            wA.append((sAC, silu_m[j - 2] if j >= 2 else prev_silu[j]))
                            h1_pe[j] = P("tensor",
                                         lambda e, slot=slot, bank=bank_h1, ybuf=ybuf:
                                         mmA(e, ybuf, 0, slot, bank),
                                         waits=wA, inc=sPE)
                            wAg = [(sDV, mult_m[j - 2] if j >= 2 else prev_mult[j])]
                            hg_pe[j] = P("tensor",
                                         lambda e, slot=slot, bank=bank_hg, ybuf=ybuf:
                                         mmA(e, ybuf, 1, slot, bank),
                                         waits=wAg, inc=sPE)
                            pairA_end[j] = sPE.n
                            if conv:
                                if j % 2 == 1 and j + 4 < 16:
                                    jj = j + 3
                                    sl0 = jj % 4
                                    wv_ = [(sPE, pairA_end[j])]
                                    if cache_pair_done[jj - 4]:
                                        wv_.append((sW, cache_pair_done[jj - 4]))
                                    v2 = P(qeng, lambda e, ei=ei, jj=jj, sl0=sl0: e.dma_start(
                                        out=wf_rg.ap().rearrange("p (s m f) -> p s m f", s=4, m=2)[:, sl0:sl0 + 2],
                                        in_=wfs[ei, jj:jj + 2].rearrange("j m p f -> p j m f")),
                                        waits=wv_, inc=qsem, k=16)
                                    pair_done[jj] = pair_done[jj + 1] = v2
                                    if jj % 4 == 2:
                                        g0 = jj - 2
                                        cache_pair_done[g0] = cache_pair_done[g0 + 1] = \
                                            cache_pair_done[g0 + 2] = cache_pair_done[g0 + 3] = P(
                                            "sync", lambda e, ei=ei, g0=g0: e.dma_start(
                                                out=cwf[ei, g0:g0 + 4].rearrange("j m p f -> p j m f"),
                                                in_=wf_rg.ap().rearrange("p (s m f) -> p s m f", s=4, m=2)),
                                            waits=[(sG, pair_done[g0 + 3])], inc=sW, k=16)
                            elif j + 4 < 16:
                                s1 = slab_src(ei, j + 4, use32)
                                wv_ = [(sPE, pairA_end[j])]
                                pair_done[j + 4] = P(qeng, lambda e, s1=s1, j=j: e.dma_start(
                                    out=wf_rg.ap().rearrange("p (s m f) -> p s m f", s=4, m=2)[:, j % 4],
                                    in_=s1), waits=wv_, inc=qsem, k=16)

                            wS = [(sPE, h1_pe[j]),
                                  (sDV, mult_m[j - 2] if j >= 2 else prev_multw[j])]
                            silu_m[j] = P("scalar", lambda e, j=j, bank=bank_h1: e.activation(
                                s_rg[:, (j % 2) * TBLK:(j % 2 + 1) * TBLK], bank, AF.Silu),
                                waits=wS, inc=sAC)
                            wM = [(sAC, silu_m[j]), (sPE, hg_pe[j])]
                            if j == 0:
                                wM.append((sPE, passB_pe_end))
                            mult_m[j] = P("vector", lambda e, j=j, bank=bank_hg: e.tensor_tensor(
                                h_sb[:, j * TBLK:(j + 1) * TBLK], bank,
                                s_rg[:, (j % 2) * TBLK:(j % 2 + 1) * TBLK], op=ALU.mult),
                                waits=wM, inc=sDV)
                        prev_silu = [silu_m[14], silu_m[15]]
                        prev_mult = [mult_m[14], mult_m[15]]
                        prev_multw = [mult_m[14], mult_m[15]]
                        prev_pairA = [pairA_end[12], pairA_end[13],
                                      pairA_end[14], pairA_end[15]]

                        if 3 <= ei <= 6 and blk + 1 < NBLK:
                            ln_vy = emit_ln_ti_pre(blk + 1, ei - 3)
                        h_done = mult_m[15]
                        sttB = [0] * (NT * 4)
                        for ti in range(NT):
                            for qi in range(4):
                                idx = ti * 4 + qi
                                col = qi * 512
                                reg = (idx % 2) * 512
                                wB = [(sDV, h_done), (sG, vW2)]
                                if idx >= 2:
                                    wB.append((sDV, sttB[idx - 2]))
                                else:
                                    wB.append((sDV, sttB_tail[idx]))

                                def doB(e, ti=ti, col=col, reg=reg):
                                    last = None
                                    for j in range(16):
                                        last = e.matmul(
                                            ps_c[:, reg:reg + 512],
                                            h_sb[:, j * TBLK + ti * 128:
                                                 j * TBLK + (ti + 1) * 128],
                                            w2_sb[:, j * D + col:j * D + col + 512],
                                            start=(j == 0), stop=(j == 15))
                                    return last
                                vB = P("tensor", doB, waits=wB, inc=sPE)
                                wstt = [(sPE, vB)]
                                if ei == 0 and blk >= 0:
                                    wstt.append((sX, vxblk))
                                sttB[idx] = P("vector",
                                    lambda e, ti=ti, col=col, reg=reg, ei=ei, wbuf=wbuf:
                                    e.scalar_tensor_tensor(
                                        x_blk[:, ti * D + col:ti * D + col + 512],
                                        ps_c[:, reg:reg + 512],
                                        wbuf[:, ti * NE + ei:ti * NE + ei + 1],
                                        x_blk[:, ti * D + col:ti * D + col + 512],
                                        op0=ALU.mult, op1=ALU.add),
                                    waits=wstt, inc=sDV)
                        sttB_tail = [sttB[NT * 4 - 2], sttB[NT * 4 - 1]]
                        passB_pe_end = sPE.n
                        if conv:
                            cache_mark[ei] = sW.n
                        if not conv:
                            cache_pair_done = [0] * 16
                            w2_cache_mark = 0

                        if 3 <= ei <= 6 and blk + 1 < NBLK:
                            yT_done[1 - par], _ = emit_ln_ti(
                                blk + 1, ei - 3, yT_bufs[1 - par], w_alls[1 - par],
                                pe_blk_end[1 - par] if ei == 3 else 0, vy=ln_vy)

                    pe_blk_end[par] = sPE.n
                    conv_blk = first and blk == 0
                    store_sem = sX if conv_blk else sG
                    store_mark = P("scalar" if conv_blk else "gpsimd",
                        lambda e, blk=blk: e.dma_start(
                            out=dst[blk * TBLK:(blk + 1) * TBLK, :].rearrange(
                                "(t p) f -> p t f", p=128),
                            in_=x_blk.ap().rearrange("p (t f) -> p t f", t=NT)),
                        waits=[(sDV, sttB[NT * 4 - 1])], inc=store_sem, k=16)
                plan.flush()
            return store_mark

        def emit_attn():
            with ExitStack() as acm:
                def sbc(name, shape, dt):
                    return acm.enter_context(nc.sbuf_tensor(name, shape, dt))
                sq_scr = sbc("sq_scr", [128, D], BF16)
                kv_bd = sbc("kv_bd", [128, 2048], BF16)
                ksum_bd = sbc("ksum_bd", [128, 16], BF16)
                r_t = sbc("r_t", [128, 1024], F32)
                m_t = sbc("m_t", [128, 1024], F32)
                xa_rg = sbc("xa_rg", [128, 2 * D], F32)
                zb_rg = sbc("zb_rg", [128, D], BF16)
                zT_rg = sbc("zT_rg", [128, 2 * D], BF16)
                bvrow_b = sbc("bvrow_b", [1, D], BF16)
                borow_b = sbc("borow_b", [1, D], BF16)
                brow_b = sbc("brow_b", [1, 2048], BF16)
                wm_b = sbc("wm_b", [128, 64], BF16)
                bT_b = sbc("bT_b", [128, 16], BF16)

                def emit_ln_pre(sl, wx, war_pe):
                    vnb = emit_ln_stats(xa_rg[:, sl * D:(sl + 1) * D], sl * 8, wx,
                                        sq_scr[:])
                    wz = [(sDV, vnb)]
                    if war_pe:
                        wz.append((sPE, war_pe))
                    vz = P("scalar", lambda e, sl=sl: e.activation(
                        zb_rg[:], xa_rg[:, sl * D:(sl + 1) * D],
                        AF.Identity, bias=st[:, sl * 8 + 6:sl * 8 + 7],
                        scale=st[:, sl * 8 + 5:sl * 8 + 6]), waits=wz, inc=sAC)
                    return vz, sAC.n

                def emit_z_trans(sl, vz, war_dv, trans_ps, halfbank=False):
                    psab = trans_ps.ap().bitcast(BF16)
                    last_ev = war_dv
                    for hg in range(2):
                        off = 0 if halfbank else hg * 1024
                        wt = [(sAC, vz)]
                        if last_ev:
                            wt.append((sDV, last_ev))

                        def dot(e, hg=hg, off=off, psab=psab):
                            last = None
                            for i in range(8):
                                cc = hg * 8 + i
                                last = e.transpose(
                                    psab[:, off + i * 128:off + (i + 1) * 128],
                                    zb_rg[:, cc * 128:(cc + 1) * 128],
                                    ident_b[:])
                            return last
                        vt = P("tensor", dot, waits=wt, inc=sPE)
                        last_ev = P("vector", lambda e, hg=hg, off=off, sl=sl, psab=psab:
                            e.tensor_copy(
                                zT_rg[:, sl * D + hg * 1024:sl * D + (hg + 1) * 1024],
                                psab[:, off:off + 1024]),
                            waits=[(sPE, vt)], inc=sDV)
                    return last_ev, sPE.n

                def emit_compose(wm_d, bT_d, wrows, brow_off, wdst_sb, wh_rg,
                                 pre_loads=()):
                    vm = P("gpsimd", lambda e: e.dma_start(out=wm_b[:], in_=wm_d), inc=sG, k=16)
                    vbb = P("gpsimd", lambda e: e.dma_start(out=bT_b[:], in_=bT_d), inc=sG, k=16)
                    quad_done = [0] * 8
                    quad_pe = [0] * 8
                    first_extra = True
                    comp_pe = {}
                    comp_dv = {}

                    def load_quad(q):
                        wv_ = [(sG, vm)]
                        if q >= 2:
                            wv_.append((sPE, quad_pe[q - 2]))
                        quad_done[q] = P("gpsimd", lambda e, q=q: e.dma_start(
                            out=wh_rg.ap().rearrange("p (s h f) -> p s h f", s=2, h=4)[:, q % 2],
                            in_=wrows[4 * q:4 * q + 4].rearrange("h p f -> p h f")),
                            waits=wv_, inc=sG, k=16)

                    load_quad(0)
                    load_quad(1)
                    drip = list(pre_loads)
                    for h in range(16):
                        if h % 4 == 0 and drip:
                            drip.pop(0)()
                        q = h // 4
                        base = (q % 2) * 4 * D + (h % 4) * D
                        wc = [(sG, quad_done[q]), (sG, vbb)]
                        if h >= 1:
                            wc.append((sDV, comp_dv[h - 1]))

                        def doc(e, base=base):
                            last = None
                            for c in range(CH):
                                last = e.matmul(ps_b[:, c * 64:(c + 1) * 64],
                                                wh_rg[:, base + c * 128:
                                                      base + (c + 1) * 128],
                                                wm_b[:], start=True, stop=True)
                            return last
                        vpc = P("tensor", doc, waits=wc, inc=sPE)
                        vpb = P("tensor", lambda e, h=h: e.matmul(
                            ps_a[0:1, h * 64:(h + 1) * 64], bT_b[:, h:h + 1], wm_b[:],
                            start=True, stop=True), inc=sPE)
                        quad_pe[q] = sPE.n
                        if h % 4 == 3 and q + 2 < 4:
                            load_quad(q + 2)
                        ev = P("vector", lambda e, h=h: e.tensor_copy(
                            wdst_sb.ap().rearrange("p (c f) -> p c f", c=CH)
                            [:, :, h * 64:(h + 1) * 64],
                            ps_b.ap().rearrange("p (c f) -> p c f", c=16)),
                            waits=[(sPE, vpc)], inc=sDV)
                        evb = P("vector", lambda e, h=h, brow_off=brow_off: e.tensor_copy(
                            brow_b[0:1, brow_off + h * 64:brow_off + (h + 1) * 64],
                            ps_a[0:1, h * 64:(h + 1) * 64]),
                            waits=[(sPE, vpb)], inc=sDV)
                        comp_pe[h] = sPE.n
                        comp_dv[h] = max(ev, evb)
                    for fn in drip:
                        fn()
                    return sPE.n, sDV.n

                with ExitStack() as a1:
                    def sb1(name, shape, dt):
                        return a1.enter_context(nc.sbuf_tensor(name, shape, dt))
                    wkm_sb = sb1("wkm_sb", [128, CH * 1024], BF16)
                    wvT_b = sb1("wvT_b", [128, CH * D], BF16)
                    wh_rg = sb1("wh_rg", [128, 8 * D], BF16)
                    km_sb = sb1("km_sb", [128, 1024], BF16)
                    v_sb = sb1("v_sb", [128, 2048], BF16)
                    kv_acc = sb1("kv_acc", [128, 1024], F32)
                    ks_acc = sb1("ks_acc", [128, 8], F32)

                    plan.barrier()
                    P("vector", lambda e: e.memset(kv_acc[:], 0.0), inc=sDV)
                    P("vector", lambda e: e.memset(ks_acc[:], 0.0), inc=sDV)
                    ld = {}

                    def _ldbv():
                        ld["vbv"] = P("gpsimd", lambda e: e.dma_start(
                            out=bvrow_b[:], in_=bv_d), inc=sG, k=16)

                    def _ldwv(cc):
                        def f():
                            ld["vwv"] = P("gpsimd", lambda e, cc=cc: e.dma_start(
                                out=wvT_b.ap().rearrange("p (c f) -> p c f", c=CH)
                                [:, 4 * cc:4 * cc + 4],
                                in_=cwvT[4 * cc:4 * cc + 4].rearrange("c p f -> p c f")),
                                inc=sG, k=16)
                        return f
                    cpe, cdv = emit_compose(wmk_d, bkT_d, cwk, 0, wkm_sb, wh_rg,
                                            pre_loads=[_ldbv] + [_ldwv(c) for c in range(4)])
                    vbv = ld["vbv"]
                    vwv = ld["vwv"]
                    if ATTN_SUB <= 1:
                        plan.flush()
                        return

                    km_ev = 0
                    v_ev_last = 0
                    v_ev = [0, 0, 0, 0]
                    kv_pe = 0
                    kv_add_dv = 0
                    zt_prev = 0
                    xa_free = {0: (0, 0), 1: (0, 0)}

                    def a1_pre(ti):
                        sl = ti % 2
                        wld = [(sAC, xa_free[sl][0]), (sDV, xa_free[sl][1])]
                        vx = P("scalar", lambda e, ti=ti, sl=sl: e.dma_start(
                            out=xa_rg[:, sl * D:(sl + 1) * D],
                            in_=xA[ti * 128:(ti + 1) * 128, :]), waits=wld, inc=sX, k=16)
                        vz, zac = emit_ln_pre(sl, [(sX, vx)], 0)
                        xa_free[sl] = (zac, sDV.n)
                        return vz

                    vz_next, _zac = None, 0
                    vz_next = a1_pre(0)
                    zt_dv, zt_pe = emit_z_trans(0, vz_next, cdv, ps_b, halfbank=True)
                    zt_prev = zt_dv
                    zt_cur = zt_dv
                    for ti in range(16):
                        sl = ti % 2
                        wkm_ = [(sDV, zt_cur)]
                        if km_ev:
                            wkm_.append((sDV, km_ev))
                        if ti + 1 < 16:
                            vz_next = a1_pre(ti + 1)

                        def dokm(e, sl=sl):
                            last = None
                            for nh in range(2):
                                col = nh * 512
                                for c in range(CH):
                                    last = e.matmul(ps_a[:, col:col + 512],
                                                    zT_rg[:, sl * D + c * 128:
                                                          sl * D + (c + 1) * 128],
                                                    wkm_sb[:, c * 1024 + col:
                                                           c * 1024 + col + 512],
                                                    start=(c == 0), stop=False)
                                last = e.matmul(ps_a[:, col:col + 512], ones_b[:],
                                                brow_b[0:1, col:col + 512],
                                                start=False, stop=True)
                            return last
                        vkm = P("tensor", dokm, waits=wkm_, inc=sPE)
                        vb_ = P("vector", lambda e: e.tensor_scalar(
                            m_t[:], ps_a[:], 0.0, None, op0=ALU.min),
                            waits=[(sPE, vkm)], inc=sDV)
                        vc = P("scalar", lambda e: e.activation(m_t[:], m_t[:], AF.Exp),
                               waits=[(sDV, vb_)], inc=sAC)
                        wke = [(sAC, vc)]
                        if ti >= 1:
                            wke.append((sPE, kv_pe))
                        km_ev = P("vector", lambda e: e.scalar_tensor_tensor(
                            km_sb[:], ps_a[:], 0.0, m_t[:], op0=ALU.max, op1=ALU.add),
                            waits=wke, inc=sDV)

                        for n in range(4):
                            vreg = ps_c[:, 1024:1536] if n % 2 == 0 else ps_b[:, 512:1024]
                            wv_ = [(sDV, zt_cur), (sG, vwv), (sG, vbv)]
                            if v_ev[n - 2]:
                                wv_.append((sAC, v_ev[n - 2]))
                            if ti >= 1 and n == 0:
                                wv_.append((sPE, kv_pe))

                            def dov(e, n=n, sl=sl, vreg=vreg):
                                for c in range(CH):
                                    e.matmul(vreg,
                                             zT_rg[:, sl * D + c * 128:sl * D + (c + 1) * 128],
                                             wvT_b[:, c * D + n * 512:c * D + n * 512 + 512],
                                             start=(c == 0), stop=False)
                                return e.matmul(vreg, ones_b[:],
                                                bvrow_b[0:1, n * 512:(n + 1) * 512],
                                                start=False, stop=True)
                            vv = P("tensor", dov, waits=wv_, inc=sPE)
                            v_ev[n] = P("scalar", lambda e, n=n, vreg=vreg: e.activation(
                                v_sb[:, n * 512:(n + 1) * 512], vreg,
                                AF.Copy), waits=[(sPE, vv)], inc=sAC)
                            v_ev_last = v_ev[n]

                        if ti + 1 < 16:
                            zt_dv, zt_pe = emit_z_trans((ti + 1) % 2, vz_next,
                                                        zt_prev, ps_b, halfbank=True)
                            zt_prev = zt_dv
                            zt_nxt = zt_dv
                        wkv = [(sDV, km_ev)] + [(sAC, v_ev[n]) for n in range(4)]
                        if ti >= 1:
                            wkv.append((sDV, kv_add_dv))

                        def dokv(e, ti=ti):
                            last = None
                            for h in range(16):
                                po = (h % 2) * 64
                                fo = (h // 2) * 128
                                last = e.matmul(ps_c[po:po + 64, fo:fo + 128],
                                                km_sb[:, h * 64:(h + 1) * 64],
                                                v_sb[:, h * 128:(h + 1) * 128],
                                                start=True, stop=True)
                            for h in range(16):
                                po = (h % 2) * 64
                                last = e.matmul(
                                    ps_c[po:po + 64, 1536 + (h // 2):1537 + (h // 2)],
                                    km_sb[:, h * 64:(h + 1) * 64], onescol_b[:],
                                    start=True, stop=True)
                            return last
                        kv_pe = P("tensor", dokv, waits=wkv, inc=sPE)
                        P("vector", lambda e: e.tensor_tensor(
                            kv_acc[:], kv_acc[:], ps_c[:, 0:1024], op=ALU.add),
                            waits=[(sPE, kv_pe)], inc=sDV)
                        kv_add_dv = P("vector", lambda e: e.tensor_tensor(
                            ks_acc[:], ks_acc[:], ps_c[:, 1536:1544], op=ALU.add),
                            inc=sDV)
                        if ti + 1 < 16:
                            zt_cur = zt_nxt
                    P("vector", lambda e: e.memset(kv_bd[:], 0.0),
                      waits=[(sDV, kv_add_dv)], inc=sDV)
                    P("vector", lambda e: e.memset(ksum_bd[:], 0.0), inc=sDV)
                    for g in range(8):
                        P("vector", lambda e, g=g: e.tensor_copy(
                            kv_bd[0:64, g * 256:g * 256 + 128],
                            kv_acc[0:64, g * 128:(g + 1) * 128]), inc=sDV)
                        P("vector", lambda e, g=g: e.tensor_copy(
                            kv_bd[64:128, g * 256 + 128:g * 256 + 256],
                            kv_acc[64:128, g * 128:(g + 1) * 128]), inc=sDV)
                        P("vector", lambda e, g=g: e.tensor_copy(
                            ksum_bd[0:64, 2 * g:2 * g + 1],
                            ks_acc[0:64, g:g + 1]), inc=sDV)
                        P("vector", lambda e, g=g: e.tensor_copy(
                            ksum_bd[64:128, 2 * g + 1:2 * g + 2],
                            ks_acc[64:128, g:g + 1]), inc=sDV)
                    plan.flush()
                if ATTN_SUB <= 2:
                    return

                a23 = acm.enter_context(ExitStack())
                qmT_sb = a23.enter_context(nc.sbuf_tensor("qmT_sb", [128, 8 * 2048], BF16))
                rn_sb = a23.enter_context(nc.sbuf_tensor("rn_sb", [128, 256], F32))
                with ExitStack() as a2:
                    def sb2(name, shape, dt):
                        return a2.enter_context(nc.sbuf_tensor(name, shape, dt))
                    wqm_sb = sb2("wqm_sb", [128, CH * 1024], BF16)
                    wh_rg2 = sb2("wh_rg2", [128, 8 * D], BF16)
                    qm_tm = sb2("qm_tm", [128, 1024], BF16)

                    plan.barrier()
                    cpe2, cdv2 = emit_compose(wmq_d, bqT_d, cwq, 1024, wqm_sb, wh_rg2)

                    qm_ev = 0
                    qmT_ev = cdv2
                    qt_pe = 0
                    zt_prev2 = 0
                    xa_free = {0: (0, 0), 1: (0, 0)}

                    def a2_pre(ti):
                        sl = ti % 2
                        wld = [(sAC, xa_free[sl][0]), (sDV, xa_free[sl][1])]
                        vx = P("scalar", lambda e, ti=ti, sl=sl: e.dma_start(
                            out=xa_rg[:, sl * D:(sl + 1) * D],
                            in_=xA[ti * 128:(ti + 1) * 128, :]), waits=wld, inc=sX, k=16)
                        vz, zac = emit_ln_pre(sl, [(sX, vx)], 0)
                        xa_free[sl] = (zac, sDV.n)
                        return vz

                    vz_next = a2_pre(0)
                    zt_cur, _ = emit_z_trans(0, vz_next, cdv2, ps_c)
                    zt_prev2 = zt_cur
                    for ti in range(16):
                        sl = ti % 2
                        wqm_ = [(sDV, zt_cur)]
                        if qm_ev:
                            wqm_.append((sDV, qm_ev))
                        if ti + 1 < 16:
                            vz_next = a2_pre(ti + 1)

                        def doqm(e, sl=sl):
                            last = None
                            for nh in range(2):
                                col = nh * 512
                                for c in range(CH):
                                    last = e.matmul(ps_a[:, col:col + 512],
                                                    zT_rg[:, sl * D + c * 128:
                                                          sl * D + (c + 1) * 128],
                                                    wqm_sb[:, c * 1024 + col:
                                                           c * 1024 + col + 512],
                                                    start=(c == 0), stop=False)
                                last = e.matmul(ps_a[:, col:col + 512], ones_b[:],
                                                brow_b[0:1, 1024 + col:1024 + col + 512],
                                                start=False, stop=True)
                            return last
                        vqm = P("tensor", doqm, waits=wqm_, inc=sPE)
                        vb_ = P("vector", lambda e: e.tensor_scalar(
                            m_t[:], ps_a[:], 0.0, None, op0=ALU.min),
                            waits=[(sPE, vqm)], inc=sDV)
                        vc = P("scalar", lambda e: e.activation(m_t[:], m_t[:], AF.Exp),
                               waits=[(sDV, vb_)], inc=sAC)
                        wqe = [(sAC, vc)]
                        if ti >= 1:
                            wqe.append((sPE, qt_pe))
                        qm_ev = P("vector", lambda e: e.scalar_tensor_tensor(
                            qm_tm[:], ps_a[:], 0.0, m_t[:], op0=ALU.max, op1=ALU.add),
                            waits=wqe, inc=sDV)

                        if ti + 1 < 16:
                            zt_nxt, _ = emit_z_trans((ti + 1) % 2, vz_next,
                                                     zt_prev2, ps_c)
                            zt_prev2 = zt_nxt
                        psbb = ps_b.ap().bitcast(BF16)

                        def doqt(e, psbb=psbb):
                            last = None
                            for g in range(8):
                                last = e.transpose(psbb[:, g * 128:(g + 1) * 128],
                                                   qm_tm[:, g * 128:(g + 1) * 128],
                                                   ident_b[:])
                            return last
                        wqt = [(sDV, qm_ev)]
                        if qmT_ev:
                            wqt.append((sDV, qmT_ev))
                        vqt = P("tensor", doqt, waits=wqt, inc=sPE)
                        qt_pe = vqt
                        qmT_ev = P("vector", lambda e, ti=ti, psbb=psbb: e.tensor_copy(
                            qmT_sb.ap().rearrange("p (g t) -> p g t", g=8)
                            [:, :, ti * 128:(ti + 1) * 128],
                            psbb[:, 0:1024].rearrange("p (g t) -> p g t", g=8)),
                            waits=[(sPE, vqt)], inc=sDV)
                        if ti + 1 < 16:
                            zt_cur = zt_nxt
                    plan.flush()
                if ATTN_SUB <= 3:
                    return

                with ExitStack() as b3:
                    def sb3(name, shape, dt):
                        return b3.enter_context(nc.sbuf_tensor(name, shape, dt))
                    woT_b = sb3("woT_b", [128, CH * D], BF16)
                    dg_sb = sb3("dg_sb", [128, 2 * 2048], BF16)
                    at_b = sb3("at_b", [128, 2048], BF16)
                    atT_b = sb3("atT_b", [128, 2048], BF16)
                    xo = sb3("xo", [128, 2 * D], F32)

                    plan.barrier()
                    vwo = P("gpsimd", lambda e: e.dma_start(
                        out=woT_b.ap().rearrange("p (c f) -> p c f", c=CH),
                        in_=cwoT.rearrange("c p f -> p c f")), inc=sG, k=16)
                    vbob = 0
                    vbo = P("gpsimd", lambda e: e.dma_start(out=borow_b[:], in_=bo_d),
                            inc=sG, k=16)

                    nrm_ev = {0: 0, 1: 0}
                    for ti in range(16):
                        reg = (ti % 2) * 512

                        def donrm_pre(e, ti=ti, reg=reg):
                            last = None
                            for g in range(8):
                                last = e.matmul(
                                    ps_b[:, reg + 2 * g:reg + 2 * g + 2],
                                    qmT_sb[:, g * 2048 + ti * 128:
                                           g * 2048 + (ti + 1) * 128],
                                    ksum_bd[:, 2 * g:2 * g + 2], start=True, stop=True)
                            return last
                        wn = []
                        if nrm_ev[ti % 2]:
                            wn.append((sDV, nrm_ev[ti % 2]))
                        vn = P("tensor", donrm_pre, waits=wn, inc=sPE)
                        P("vector", lambda e, ti=ti, reg=reg: e.tensor_scalar(
                            rn_sb[:, ti * 16:(ti + 1) * 16], ps_b[:, reg:reg + 16],
                            AT_EPS, None, op0=ALU.add), waits=[(sPE, vn)], inc=sDV)
                        nrm_ev[ti % 2] = P("vector", lambda e, ti=ti: e.reciprocal(
                            rn_sb[:, ti * 16:(ti + 1) * 16],
                            rn_sb[:, ti * 16:(ti + 1) * 16]), inc=sDV)
                    rn_done = sDV.n

                    at_ac = 0
                    atT_dv = 0
                    vat_prev = 0
                    dg_mark = {0: 0, 1: 0}
                    o_pe_last = 0
                    o_half_dv = {0: 0, 1: 0}
                    o_dv_sl = {0: 0, 1: 0}
                    st_sl = {0: 0, 1: 0}

                    def donum(e, ti):
                        last = None
                        for g in range(8):
                            last = e.matmul(
                                ps_c[:, g * 256:(g + 1) * 256],
                                qmT_sb[:, g * 2048 + ti * 128:
                                       g * 2048 + (ti + 1) * 128],
                                kv_bd[:, g * 256:(g + 1) * 256],
                                start=True, stop=True)
                        return last

                    vnum = P("tensor", lambda e: donum(e, 0),
                             waits=[(sDV, rn_done)], inc=sPE)
                    for ti in range(16):
                        sl = ti % 2
                        wld = [(sDV, o_dv_sl[sl])]
                        vx = P("scalar", lambda e, ti=ti, sl=sl: e.dma_start(
                            out=xa_rg[:, sl * D:(sl + 1) * D],
                            in_=xA[ti * 128:(ti + 1) * 128, :]), waits=wld, inc=sX, k=16)

                        wat = [(sPE, vnum)]
                        if vat_prev:
                            wat.append((sPE, vat_prev))
                        at_ac = P("scalar", lambda e: e.activation(
                            at_b[:], ps_c[:], AF.Identity), waits=wat, inc=sAC)
                        if ti == 0:
                            for tb in (0, 1):
                                wdg = [(sDV, rn_done)]
                                for h in range(16):
                                    P("scalar", lambda e, tb=tb, h=h: e.activation(
                                        dg_sb[:, tb * 2048 + h * 128:tb * 2048 + (h + 1) * 128],
                                        ident_b[:], AF.Identity,
                                        scale=rn_sb[:, tb * 16 + h:tb * 16 + h + 1]),
                                        waits=(wdg if h == 0 else []), inc=sAC)
                                dg_mark[tb] = sAC.n
                        elif ti + 1 < 16:
                            tb = (ti + 1) % 2
                            wdg = [(sDV, rn_done), (sPE, vat_prev)]
                            for h in range(16):
                                P("scalar", lambda e, ti=ti, tb=tb, h=h: e.activation(
                                    dg_sb[:, tb * 2048 + h * 128:tb * 2048 + (h + 1) * 128],
                                    ident_b[:], AF.Identity,
                                    scale=rn_sb[:, (ti + 1) * 16 + h:(ti + 1) * 16 + h + 1]),
                                    waits=(wdg if h == 0 else []), inc=sAC)
                            dg_mark[tb] = sAC.n
                        dg_dv = dg_mark[ti % 2]
                        dgo = (ti % 2) * 2048

                        vat_h = [0, 0]
                        atT_half = [0, 0]
                        for rnd in range(2):
                            def doatt(e, rnd=rnd, dgo=dgo):
                                last = None
                                for cc in range(8):
                                    b = rnd * 8 + cc
                                    last = e.matmul(
                                        ps_a[:, cc * 128:(cc + 1) * 128],
                                        at_b[:, b * 128:(b + 1) * 128],
                                        dg_sb[:, dgo + b * 128:dgo + (b + 1) * 128],
                                        start=True, stop=True)
                                return last
                            watt = [(sAC, at_ac), (sAC, dg_dv)]
                            if atT_dv:
                                watt.append((sDV, atT_dv))
                            if rnd == 1:
                                watt = [(sDV, atT_half[0])]
                            vat_h[rnd] = P("tensor", doatt, waits=watt, inc=sPE)
                            atT_half[rnd] = P("vector", lambda e, rnd=rnd: e.tensor_copy(
                                atT_b[:, rnd * 1024:(rnd + 1) * 1024], ps_a[:]),
                                waits=[(sPE, vat_h[rnd])], inc=sDV)
                        vat = vat_h[1]
                        vat_prev = vat
                        atT_dv = atT_half[1]

                        if ti + 1 < 16:
                            vnum = P("tensor", lambda e, ti=ti: donum(e, ti + 1),
                                     waits=[(sAC, at_ac)], inc=sPE)

                        for n in range(4):
                            half = n % 2
                            wo_ = [(sDV, atT_dv), (sG, vwo), (sG, vbo),
                                   (sDV, o_half_dv[half] or rn_done)]

                            def doo(e, n=n, half=half):
                                for c in range(CH):
                                    e.matmul(ps_b[:, half * 512:half * 512 + 512],
                                             atT_b[:, c * 128:(c + 1) * 128],
                                             woT_b[:, c * D + n * 512:c * D + n * 512 + 512],
                                             start=(c == 0), stop=False)
                                return e.matmul(ps_b[:, half * 512:half * 512 + 512],
                                                ones_b[:],
                                                borow_b[0:1, n * 512:(n + 1) * 512],
                                                start=False, stop=True)
                            vo = P("tensor", doo, waits=wo_, inc=sPE)
                            wvo_ = [(sPE, vo), (sX, vx)]
                            if st_sl[sl]:
                                wvo_.append((sX, st_sl[sl]))
                            o_half_dv[half] = P("vector",
                                lambda e, n=n, half=half, sl=sl: e.tensor_tensor(
                                    xo[:, sl * D + n * 512:sl * D + (n + 1) * 512],
                                    ps_b[:, half * 512:half * 512 + 512],
                                    xa_rg[:, sl * D + n * 512:sl * D + (n + 1) * 512],
                                    op=ALU.add), waits=wvo_, inc=sDV)
                        o_pe_last = sPE.n
                        o_dv_sl[sl] = sDV.n
                        st_sl[sl] = P("scalar", lambda e, ti=ti, sl=sl: e.dma_start(
                            out=xB[ti * 128:(ti + 1) * 128, :],
                            in_=xo[:, sl * D:(sl + 1) * D]),
                            waits=[(sDV, o_dv_sl[sl])], inc=sX, k=16)
                    plan.flush()

        emit_ffn(0, hs, xA)
        if phases >= 2:
            emit_attn()
        if phases >= 3:
            final_store = emit_ffn(1, xB, out_d)
        P("sync", lambda e: e.nop(),
          waits=[(sX, sX.n), (sW, sW.n), (sG, sG.n), (sPE, sPE.n),
                 (sDV, sDV.n), (sAC, sAC.n)])
        plan.flush()
    return nc


def _relayout(inputs):
    import ml_dtypes

    def g(k):
        return np.ascontiguousarray(np.asarray(inputs[k], dtype=np.float32))

    hs_all = g("L0_attn_hidden_states")
    w1_all = np.concatenate([g("L0_ffn_Ws1")[None], g("L0_ffn_W1")], 0)
    wg_all = np.concatenate([g("L0_ffn_Wsg")[None], g("L0_ffn_Wg")], 0)
    w2_all = np.concatenate([g("L0_ffn_Ws2")[None], g("L0_ffn_W2")], 0)
    w1s = w1_all.reshape(NE, 16, 128, 16, 128).transpose(0, 3, 2, 1, 4).reshape(
        NE, 16, 128, D)
    wgs = wg_all.reshape(NE, 16, 128, 16, 128).transpose(0, 3, 2, 1, 4).reshape(
        NE, 16, 128, D)
    wfs = np.ascontiguousarray(np.stack([w1s, wgs], axis=2))
    w2s = np.ascontiguousarray(w2_all.reshape(NE, 16, 128, D))
    wq = g("L0_attn_wq")
    wk = g("L0_attn_wk")
    wv = g("L0_attn_wv")
    wo = g("L0_attn_wo")
    com = dict(
        wfs=wfs, w2s=w2s,
        wq_r=np.ascontiguousarray(wq.reshape(16, 128, D)),
        wk_r=np.ascontiguousarray(wk.reshape(16, 128, D)),
        wvT_d=np.ascontiguousarray(wv.T).reshape(16, 128, D),
        woT_d=np.ascontiguousarray(wo.T).reshape(16, 128, D),
        wmq_d=np.ascontiguousarray(g("L0_attn_wmap_q").T),
        wmk_d=np.ascontiguousarray(g("L0_attn_wmap_k").T),
        bqT_d=np.ascontiguousarray(g("L0_attn_bq").reshape(16, 128).T),
        bkT_d=np.ascontiguousarray(g("L0_attn_bk").reshape(16, 128).T),
        bv_d=g("L0_attn_bv").reshape(1, D),
        bo_d=g("L0_attn_bo").reshape(1, D),
        gw_d=np.ascontiguousarray(
            g("L0_ffn_gate_w").reshape(16, 128, 7).transpose(1, 0, 2)).reshape(128, 112),
        gb_d=g("L0_ffn_gate_b").reshape(1, 7),
        be_d=g("L0_ffn_biases").reshape(1, 7),
        idf_d=np.eye(128, dtype=np.float32),
        idb_d=np.eye(128, dtype=np.float32).astype(ml_dtypes.bfloat16),
    )
    return hs_all, com


def kernel(**inputs):
    hs_all, com = _relayout(inputs)
    nc = build()
    in_maps = [dict(com, hs=np.ascontiguousarray(hs_all[i])) for i in range(8)]
    res = run_bass_kernel_spmd(nc, in_maps, core_ids=list(range(8)))
    out = np.stack([np.asarray(res.results[i]["out"]) for i in range(8)], 0)
    return out.astype(np.float32)


if __name__ == "__main__":
    nc = build()
    n = sum(1 for _ in nc.all_instructions())
    print("built ok, instructions:", n)


# version 13
# speedup vs baseline: 1.0799x; 1.0799x over previous
import numpy as np
from contextlib import ExitStack

import concourse.bass as bass
import concourse.mybir as mybir
from concourse.bass_utils import run_bass_kernel_spmd

F32 = mybir.dt.float32
BF16 = mybir.dt.bfloat16
AF = mybir.ActivationFunctionType
ALU = mybir.AluOpType
AX = mybir.AxisListType

D = 2048
S = 2048
TBLK = 512
NT = TBLK // 128
NBLK = S // TBLK
NE = 8
CH = 16
LN_EPS = 1e-5
AT_EPS = 1e-6
BIG = 1e30
SERIAL = False
SER_SECTIONS = ""
ATTN_SUB = 7

ENGINES = ("sync", "gpsimd", "vector", "scalar", "tensor")


class Sem:
    def __init__(self, ctx, nc, name):
        self.h = ctx.enter_context(nc.semaphore(name))
        self.n = 0


class Plan:
    def __init__(self, block, sems):
        self.block = block
        self.sems = sems
        self.ops = {e: [] for e in ENGINES}
        self.force_serial = False
        self.last_inc = {}

    def op(self, eng, fn, waits=(), inc=None, k=1):
        if SERIAL or self.force_serial:
            waits = [(s, s.n) for s in self.sems]
        else:
            waits = list(waits)
            auto = self.last_inc.get((eng, k == 16))
            if auto is not None:
                waits.append(auto)
        if inc is not None:
            inc.n += k
            v = inc.n
            self.last_inc[(eng, k == 16)] = (inc, v)
        else:
            v = None
        best = {}
        for s, w in waits:
            if w > 0 and (s not in best or w > best[s]):
                best[s] = w
        ws = tuple(best.items())
        self.ops[eng].append((fn, ws, inc, k))
        return v

    def barrier(self):
        snap = [(s, s.n) for s in self.sems]
        for e in ENGINES:
            self.ops[e].append((None, tuple((s, w) for s, w in snap if w > 0), None, 0))

    def flush(self):
        for ename in ENGINES:
            lst = self.ops[ename]
            if not lst:
                continue

            def body(eng, lst=lst):
                for fn, waits, inc, k in lst:
                    for sem, v in waits:
                        eng.wait_ge(sem.h, v)
                    ins = fn(eng) if fn is not None else None
                    if inc is not None:
                        (ins if ins is not None else eng.nop()).then_inc(inc.h, k)

            getattr(self.block, ename)(body)
        self.ops = {e: [] for e in ENGINES}


def build(dump=(), phases=3):
    nc = bass.Bass("TRN2", target_bir_lowering=False)

    hs = nc.dram_tensor("hs", [S, D], F32, kind="ExternalInput").ap()
    wfs = nc.dram_tensor("wfs", [NE, 16, 2, 128, D], F32, kind="ExternalInput").ap()
    w2s = nc.dram_tensor("w2s", [NE, 16, 128, D], F32, kind="ExternalInput").ap()
    wq_r = nc.dram_tensor("wq_r", [16, 128, D], F32, kind="ExternalInput").ap()
    wk_r = nc.dram_tensor("wk_r", [16, 128, D], F32, kind="ExternalInput").ap()
    wvT_d = nc.dram_tensor("wvT_d", [16, 128, D], F32, kind="ExternalInput").ap()
    woT_d = nc.dram_tensor("woT_d", [16, 128, D], F32, kind="ExternalInput").ap()
    wmq_d = nc.dram_tensor("wmq_d", [128, 64], F32, kind="ExternalInput").ap()
    wmk_d = nc.dram_tensor("wmk_d", [128, 64], F32, kind="ExternalInput").ap()
    bqT_d = nc.dram_tensor("bqT_d", [128, 16], F32, kind="ExternalInput").ap()
    bkT_d = nc.dram_tensor("bkT_d", [128, 16], F32, kind="ExternalInput").ap()
    bv_d = nc.dram_tensor("bv_d", [1, D], F32, kind="ExternalInput").ap()
    bo_d = nc.dram_tensor("bo_d", [1, D], F32, kind="ExternalInput").ap()
    gw_d = nc.dram_tensor("gw_d", [128, 112], F32, kind="ExternalInput").ap()
    gb_d = nc.dram_tensor("gb_d", [1, 7], F32, kind="ExternalInput").ap()
    be_d = nc.dram_tensor("be_d", [1, 7], F32, kind="ExternalInput").ap()
    idf_d = nc.dram_tensor("idf_d", [128, 128], F32, kind="ExternalInput").ap()
    idb_d = nc.dram_tensor("idb_d", [128, 128], BF16, kind="ExternalInput").ap()

    out_d = nc.dram_tensor("out", [S, D], F32, kind="ExternalOutput").ap()
    xA = nc.dram_tensor("xA", [S, D], F32,
                        kind="ExternalOutput" if "xA" in dump else "Internal").ap()
    xB = nc.dram_tensor("xB", [S, D], F32,
                        kind="ExternalOutput" if "xB" in dump else "Internal").ap()
    cwf = nc.dram_tensor("cwf", [NE, 16, 2, 128, D], BF16, kind="Internal").ap()
    cw2 = nc.dram_tensor("cw2", [NE, 16, 128, D], BF16, kind="Internal").ap()
    cwq = nc.dram_tensor("cwq", [16, 128, D], BF16, kind="Internal").ap()
    cwk = nc.dram_tensor("cwk", [16, 128, D], BF16, kind="Internal").ap()
    cwvT = nc.dram_tensor("cwvT", [16, 128, D], BF16, kind="Internal").ap()
    cwoT = nc.dram_tensor("cwoT", [16, 128, D], BF16, kind="Internal").ap()

    with ExitStack() as ctx:
        block = ctx.enter_context(nc.Block())
        sW = Sem(ctx, nc, "sW")
        sG = Sem(ctx, nc, "sG")
        sX = Sem(ctx, nc, "sX")
        sPE = Sem(ctx, nc, "sPE")
        sDV = Sem(ctx, nc, "sDV")
        sAC = Sem(ctx, nc, "sAC")
        allsems = (sW, sG, sX, sPE, sDV, sAC)

        ident_f = ctx.enter_context(nc.sbuf_tensor("ident_f", [128, 128], F32))
        ident_b = ctx.enter_context(nc.sbuf_tensor("ident_b", [128, 128], BF16))
        gw_sb = ctx.enter_context(nc.sbuf_tensor("gw_sb", [128, 112], F32))
        gb_sb = ctx.enter_context(nc.sbuf_tensor("gb_sb", [1, 7], F32))
        be_bc = ctx.enter_context(nc.sbuf_tensor("be_bc", [128, 7], F32))
        ones_f = ctx.enter_context(nc.sbuf_tensor("ones_f", [1, 128], F32))
        ones_b = ctx.enter_context(nc.sbuf_tensor("ones_b", [1, 128], BF16))
        onescol_b = ctx.enter_context(nc.sbuf_tensor("onescol_b", [128, 1], BF16))
        st = ctx.enter_context(nc.sbuf_tensor("st", [128, 16], F32))
        g_e = ctx.enter_context(nc.sbuf_tensor("g_e", [128, 7], F32))
        g_p = ctx.enter_context(nc.sbuf_tensor("g_p", [128, 7], F32))
        g_b = ctx.enter_context(nc.sbuf_tensor("g_b", [128, 7], F32))
        g_b2 = ctx.enter_context(nc.sbuf_tensor("g_b2", [128, 7], F32))
        g_t1 = ctx.enter_context(nc.sbuf_tensor("g_t1", [128, 7], F32))
        g_t2 = ctx.enter_context(nc.sbuf_tensor("g_t2", [128, 7], F32))
        g_s = ctx.enter_context(nc.sbuf_tensor("g_s", [128, 8], F32))
        w_all = ctx.enter_context(nc.sbuf_tensor("w_all", [128, NT * NE], F32))

        ps_a = ctx.enter_context(nc.psum_tensor("ps_a", [128, 1024], F32))
        ps_b = ctx.enter_context(nc.psum_tensor("ps_b", [128, 1024], F32))
        ps_c = ctx.enter_context(nc.psum_tensor("ps_c", [128, 2048], F32))

        plan = Plan(block, allsems)
        P = plan.op

        P("sync", lambda e: e.dma_start(out=ident_f[:], in_=idf_d), inc=sW, k=16)
        P("sync", lambda e: e.dma_start(out=ident_b[:], in_=idb_d), inc=sW, k=16)
        P("sync", lambda e: e.dma_start(out=gw_sb[:], in_=gw_d), inc=sW, k=16)
        P("sync", lambda e: e.dma_start(out=gb_sb[:], in_=gb_d), inc=sW, k=16)
        vbe = P("sync", lambda e: e.dma_start(out=g_e[0:1, :], in_=be_d), inc=sW, k=16)
        P("vector", lambda e: e.memset(ones_f[:], 1.0), inc=sDV)
        P("vector", lambda e: e.memset(ones_b[:], 1.0), inc=sDV)
        vms = P("vector", lambda e: e.memset(onescol_b[:], 1.0), inc=sDV)
        vpe = P("tensor", lambda e: e.matmul(ps_a[:, 0:7], ones_f[:], g_e[0:1, :],
                                             start=True, stop=True),
                waits=[(sW, vbe), (sDV, vms)], inc=sPE)
        P("vector", lambda e: e.tensor_copy(be_bc[:], ps_a[:, 0:7]),
          waits=[(sPE, vpe)], inc=sDV)
        plan.barrier()

        def emit_ln_stats(xa_ap, base, wx, sq_out, sq_waits=()):
            vsq = P("scalar", lambda e: e.activation(
                sq_out, xa_ap, AF.Square, accum_out=st[:, base + 1:base + 2]),
                waits=list(wx) + list(sq_waits), inc=sAC)
            P("vector", lambda e: e.tensor_reduce(
                st[:, base:base + 1], xa_ap, axis=AX.X, op=ALU.add), waits=wx, inc=sDV)
            P("vector", lambda e: e.tensor_scalar(
                st[:, base + 2:base + 3], st[:, base:base + 1], 1.0 / D, None,
                op0=ALU.mult), waits=[(sAC, vsq)], inc=sDV)
            P("vector", lambda e: e.tensor_scalar(
                st[:, base + 3:base + 4], st[:, base + 1:base + 2], 1.0 / D, LN_EPS,
                op0=ALU.mult, op1=ALU.add), inc=sDV)
            P("vector", lambda e: e.tensor_tensor(
                st[:, base + 4:base + 5], st[:, base + 2:base + 3],
                st[:, base + 2:base + 3], op=ALU.mult), inc=sDV)
            vvar = P("vector", lambda e: e.tensor_tensor(
                st[:, base + 3:base + 4], st[:, base + 3:base + 4],
                st[:, base + 4:base + 5], op=ALU.subtract), inc=sDV)
            vsd = P("scalar", lambda e: e.activation(
                st[:, base + 4:base + 5], st[:, base + 3:base + 4], AF.Sqrt),
                waits=[(sDV, vvar)], inc=sAC)
            P("vector", lambda e: e.reciprocal(
                st[:, base + 5:base + 6], st[:, base + 4:base + 5]),
                waits=[(sAC, vsd)], inc=sDV)
            vnb = P("vector", lambda e: e.scalar_tensor_tensor(
                st[:, base + 6:base + 7], st[:, base + 2:base + 3], -1.0,
                st[:, base + 5:base + 6], op0=ALU.mult, op1=ALU.mult), inc=sDV)
            return vnb

        def emit_ffn(call, src, dst):
            first = (call == 0)
            with ExitStack() as fs:
                def sb(name, shape, dt):
                    return fs.enter_context(nc.sbuf_tensor(f"{name}_{call}", shape, dt))
                x_blk = sb("x_blk", [128, NT * D], F32)
                x_ln = sb("x_ln", [128, D], F32)
                yT_bufs = [sb("yT_bA", [128, CH * TBLK], BF16),
                           sb("yT_bB", [128, CH * TBLK], BF16)]
                w_alls = [sb("w_allA", [128, NT * NE], F32),
                          sb("w_allB", [128, NT * NE], F32)]
                yT32 = sb("yT32", [128, D], F32)
                y_tm = sb("y_tm", [128, D], F32)
                h_sb = sb("h_sb", [128, 16 * TBLK], BF16)
                w2_sb = sb("w2_sb", [128, CH * D], BF16)
                wf_rg = sb("wf_rg", [128, 2 * 2 * 2 * D], BF16)
                s_rg = sb("s_rg", [128, 2 * TBLK], F32)

                def mmA(e, ybuf, mat, slot, bank):
                    base = slot * 2 * D + mat * D
                    last = None
                    for c in range(CH):
                        last = e.matmul(
                            bank,
                            wf_rg[:, base + c * 128:base + (c + 1) * 128],
                            ybuf[:, c * TBLK:(c + 1) * TBLK],
                            start=(c == 0), stop=(c == CH - 1))
                    return last

                plan.barrier()
                cache_mark = [0] * NE
                w2_cache_mark = 0
                prev_silu = [0, 0]
                prev_mult = [0, 0]
                prev_pairA = [0, 0, 0, 0]
                prev_multw = [0, 0]
                passB_pe_end = 0
                sttB_tail = [0, 0]
                cache_pair_done = [0] * 16
                store_mark = 0
                store_sem = sX
                pe_blk_end = [0, 0]
                M = dict(xln_ac={0: 0, 1: 0}, xln_dv=0,
                         ytm_pe=0,
                         b_free={2: (0, 0), 3: (0, 0)},
                         y32_pe=0,
                         vexp=0)

                def emit_ln_ti_pre(blk, ti):
                    t0 = blk * TBLK + ti * 128
                    base = (ti % 2) * 8
                    vx = P("scalar", lambda e, t0=t0: e.dma_start(
                        out=x_ln[:], in_=src[t0:t0 + 128, :]),
                        waits=[(sAC, M["xln_ac"][0]), (sDV, M["xln_dv"])], inc=sX, k=16)
                    vnb = emit_ln_stats(x_ln[:], base, [(sX, vx)], y_tm[:],
                                        [(sPE, M["ytm_pe"])])
                    wn = [(sDV, vnb)]
                    vy = P("scalar", lambda e, base=base: e.activation(
                        y_tm[:], x_ln[:],
                        AF.Identity, bias=st[:, base + 6:base + 7],
                        scale=st[:, base + 5:base + 6]), waits=wn, inc=sAC)
                    M["xln_ac"][0] = vy
                    M["xln_dv"] = vnb
                    return vy

                def emit_ln_ti(blk, ti, ybuf, wbuf, war_pe, vy=None):
                    if True:
                        base = (ti % 2) * 8
                        sl = ti % 2
                        if vy is None:
                            vy = emit_ln_ti_pre(blk, ti)
                        for g in range(4):
                            bk = 2 + (g % 2)
                            reg = 1024 + (g % 2) * 512
                            wt = [(sAC, vy)]
                            fdv, fac = M["b_free"][bk]
                            if fdv:
                                wt.append((sDV, fdv))
                            if fac:
                                wt.append((sAC, fac))
                            if bk == 2:
                                wt.append((sAC, M["vexp"]))
                            if g == 0 and war_pe:
                                wt.append((sPE, war_pe))
                            if g == 0:
                                wt.append((sPE, M["y32_pe"]))

                            def dot(e, g=g, reg=reg):
                                last = None
                                for i in range(4):
                                    last = e.transpose(
                                        ps_c[:, reg + i * 128:reg + (i + 1) * 128],
                                        y_tm[:, (g * 4 + i) * 128:(g * 4 + i + 1) * 128],
                                        ident_f[:])
                                return last
                            vt = P("tensor", dot, waits=wt, inc=sPE)
                            v1 = P("vector", lambda e, g=g, reg=reg: e.tensor_copy(
                                yT32[:, g * 512:(g + 1) * 512],
                                ps_c[:, reg:reg + 512]), waits=[(sPE, vt)], inc=sDV)
                            vc = P("scalar", lambda e, g=g, reg=reg, ti=ti, ybuf=ybuf: e.activation(
                                ybuf.ap().rearrange("p (c t) -> p c t", c=CH)
                                [:, 4 * g:4 * g + 4, ti * 128:(ti + 1) * 128],
                                ps_c.ap().rearrange("p (c t) -> p c t", c=16)
                                [:, (reg // 128):(reg // 128) + 4, :],
                                AF.Copy), waits=[(sPE, vt), (sDV, v1)] +
                                ([(sPE, war_pe)] if (g == 0 and ti == 0 and war_pe) else []),
                                inc=sAC)
                            M["b_free"][bk] = (v1, vc)
                        M["ytm_pe"] = sPE.n
                        yT32_done = sDV.n

                        fdv, fac = M["b_free"][2]
                        wg_ = [(sDV, yT32_done), (sDV, fdv), (sAC, fac)]

                        def dogate(e):
                            for c in range(CH):
                                e.matmul(ps_c[:, 1024:1031],
                                         yT32[:, c * 128:(c + 1) * 128],
                                         gw_sb[:, c * 7:(c + 1) * 7],
                                         start=(c == 0), stop=False)
                            return e.matmul(ps_c[:, 1024:1031], ones_f[:], gb_sb[:],
                                            start=False, stop=True)
                        vgm = P("tensor", dogate, waits=wg_, inc=sPE)
                        M["y32_pe"] = vgm
                        P("vector", lambda e: e.tensor_reduce(
                            g_s[:, 0:1], ps_c[:, 1024:1031], axis=AX.X, op=ALU.max),
                            waits=[(sPE, vgm)], inc=sDV)
                        vneg = P("vector", lambda e: e.tensor_scalar(
                            g_s[:, 1:2], g_s[:, 0:1], -1.0, None, op0=ALU.mult), inc=sDV)
                        vexp = P("scalar", lambda e: e.activation(
                            g_e[:], ps_c[:, 1024:1031], AF.Exp, bias=g_s[:, 1:2]),
                            waits=[(sDV, vneg)], inc=sAC)
                        M["vexp"] = vexp
                        P("vector", lambda e: e.tensor_reduce(
                            g_s[:, 2:3], g_e[:], axis=AX.X, op=ALU.add),
                            waits=[(sAC, vexp)], inc=sDV)
                        P("vector", lambda e: e.reciprocal(g_s[:, 3:4], g_s[:, 2:3]), inc=sDV)
                        P("vector", lambda e: e.tensor_scalar(
                            g_p[:], g_e[:], g_s[:, 3:4], None, op0=ALU.mult), inc=sDV)
                        P("vector", lambda e: e.tensor_tensor(
                            g_b[:], g_p[:], be_bc[:], op=ALU.add), inc=sDV)
                        P("vector", lambda e: e.tensor_reduce(
                            g_s[:, 4:5], g_b[:], axis=AX.X, op=ALU.max), inc=sDV)
                        P("vector", lambda e: e.tensor_scalar(
                            g_t1[:], g_b[:], g_s[:, 4:5], None, op0=ALU.is_ge), inc=sDV)
                        P("vector", lambda e: e.scalar_tensor_tensor(
                            g_b2[:], g_t1[:], -BIG, g_b[:], op0=ALU.mult, op1=ALU.add), inc=sDV)
                        P("vector", lambda e: e.tensor_reduce(
                            g_s[:, 4:5], g_b2[:], axis=AX.X, op=ALU.max), inc=sDV)
                        P("vector", lambda e: e.tensor_scalar(
                            g_t2[:], g_b2[:], g_s[:, 4:5], None, op0=ALU.is_ge), inc=sDV)
                        P("vector", lambda e: e.scalar_tensor_tensor(
                            g_b2[:], g_t2[:], -BIG, g_b2[:], op0=ALU.mult, op1=ALU.add), inc=sDV)
                        P("vector", lambda e: e.tensor_reduce(
                            g_s[:, 4:5], g_b2[:], axis=AX.X, op=ALU.max), inc=sDV)
                        P("vector", lambda e: e.tensor_scalar(
                            g_b2[:], g_b2[:], g_s[:, 4:5], None, op0=ALU.is_ge), inc=sDV)
                        P("vector", lambda e: e.tensor_tensor(
                            g_t1[:], g_t1[:], g_t2[:], op=ALU.add), inc=sDV)
                        P("vector", lambda e: e.tensor_tensor(
                            g_t1[:], g_t1[:], g_b2[:], op=ALU.add), inc=sDV)
                        P("vector", lambda e: e.tensor_tensor(
                            g_p[:], g_p[:], g_t1[:], op=ALU.mult), inc=sDV)
                        P("vector", lambda e: e.tensor_reduce(
                            g_s[:, 5:6], g_p[:], axis=AX.X, op=ALU.add), inc=sDV)
                        P("vector", lambda e: e.reciprocal(g_s[:, 6:7], g_s[:, 5:6]), inc=sDV)
                        P("vector", lambda e: e.tensor_scalar(
                            g_s[:, 7:8], g_s[:, 6:7], 0.5, None, op0=ALU.mult), inc=sDV)
                        if ti == 0:
                            P("vector", lambda e, wbuf=wbuf: e.memset(wbuf[:], 0.5), inc=sDV)
                        P("vector", lambda e, ti=ti, wbuf=wbuf: e.tensor_scalar(
                            wbuf[:, ti * NE + 1:(ti + 1) * NE], g_p[:], g_s[:, 7:8],
                            None, op0=ALU.mult), inc=sDV)
                    return sAC.n, sDV.n

                def slab_src(eidx, j, use32):
                    t = wfs if use32 else cwf
                    return t[eidx, j].rearrange("m p f -> p m f")

                yT_done = [0, 0]
                for ti0 in range(NT):
                    yT_done[0], _ = emit_ln_ti(0, ti0, yT_bufs[0], w_alls[0], 0)

                for blk in range(NBLK):
                    par = blk % 2
                    ybuf = yT_bufs[par]
                    wbuf = w_alls[par]
                    ln_act_last = yT_done[par]
                    wxb = [] if blk == 0 else [(store_sem, store_mark)]
                    vxblk = P("scalar", lambda e, blk=blk: e.dma_start(
                        out=x_blk.ap().rearrange("p (t f) -> p t f", t=NT),
                        in_=src[blk * TBLK:(blk + 1) * TBLK, :].rearrange(
                            "(t p) f -> p t f", p=128)), waits=wxb, inc=sX, k=16)

                    for ei in range(NE):
                        use32 = first and blk == 0
                        conv = use32
                        qeng = "gpsimd" if use32 else "sync"
                        qsem = sG if use32 else sW
                        w2src = w2s[ei] if use32 else cw2[ei]
                        ww2 = [(sPE, passB_pe_end)]
                        if w2_cache_mark:
                            ww2.append((sW, w2_cache_mark))
                        if not use32 and first:
                            ww2.append((sW, cache_mark[ei]))
                        if not conv:
                            vW2 = P("gpsimd", lambda e, w2src=w2src: e.dma_start(
                                out=w2_sb.ap().rearrange("p (j f) -> p j f", j=16),
                                in_=w2src.rearrange("j p f -> p j f")), waits=ww2, inc=sG, k=16)
                        if first and blk >= 2 and ei in (1, 3):
                            srcT, dstT = {(2, 1): (wq_r, cwq), (2, 3): (wk_r, cwk),
                                          (3, 1): (wvT_d, cwvT), (3, 3): (woT_d, cwoT)}[(blk, ei)]
                            P("gpsimd", lambda e, srcT=srcT, dstT=dstT: e.dma_start(
                                out=dstT, in_=srcT), inc=sG, k=16)

                        pair_done = [0] * 16
                        pairA_end = [0] * 16
                        if not conv:
                            t_ = wfs if use32 else cwf
                            wv_ = [(sPE, max(prev_pairA))]
                            cpd = max(cache_pair_done[12:16])
                            if cpd:
                                wv_.append((sW, cpd))
                            if not use32 and first:
                                wv_.append((sW, cache_mark[ei]))
                            v4 = P(qeng, lambda e, t_=t_, ei=ei: e.dma_start(
                                out=wf_rg.ap().rearrange("p (s m f) -> p s m f", s=4, m=2),
                                in_=t_[ei, 0:4].rearrange("j m p f -> p j m f")),
                                waits=wv_, inc=qsem, k=16)
                            pair_done[0] = pair_done[1] = pair_done[2] = pair_done[3] = v4
                        else:
                            for jp in (0, 2):
                                wv_ = [(sPE, max(prev_pairA))]
                                cpd = max(cache_pair_done[12:16])
                                if cpd:
                                    wv_.append((sW, cpd))
                                v2 = P(qeng, lambda e, ei=ei, jp=jp: e.dma_start(
                                    out=wf_rg.ap().rearrange("p (s m f) -> p s m f", s=4, m=2)[:, jp:jp + 2],
                                    in_=wfs[ei, jp:jp + 2].rearrange("j m p f -> p j m f")),
                                    waits=wv_, inc=qsem, k=16)
                                pair_done[jp] = pair_done[jp + 1] = v2
                            cache_pair_done[0] = cache_pair_done[1] = \
                                cache_pair_done[2] = cache_pair_done[3] = P(
                                "sync", lambda e, ei=ei: e.dma_start(
                                    out=cwf[ei, 0:4].rearrange("j m p f -> p j m f"),
                                    in_=wf_rg.ap().rearrange("p (s m f) -> p s m f", s=4, m=2)),
                                waits=[(sG, pair_done[3])], inc=sW, k=16)

                        if conv:
                            for hf in range(2):
                                vW2 = P("gpsimd", lambda e, w2src=w2src, hf=hf: e.dma_start(
                                    out=w2_sb.ap().rearrange("p (j f) -> p j f", j=16)[:, hf * 8:(hf + 1) * 8],
                                    in_=w2src[hf * 8:(hf + 1) * 8].rearrange("j p f -> p j f")),
                                    waits=ww2 if hf == 0 else [], inc=sG, k=16)
                            w2_cache_mark = P("sync", lambda e, ei=ei: e.dma_start(
                                out=cw2[ei].rearrange("j p f -> p j f"),
                                in_=w2_sb.ap().rearrange("p (j f) -> p j f", j=16)),
                                waits=[(sG, vW2)], inc=sW, k=16)

                        silu_m = [0] * 16
                        mult_m = [0] * 16
                        h1_pe = [0] * 16
                        hg_pe = [0] * 16
                        for j in range(16):
                            slot = j % 4
                            bank_h1 = ps_a[:, (j % 2) * 512:(j % 2) * 512 + 512]
                            bank_hg = ps_b[:, (j % 2) * 512:(j % 2) * 512 + 512]
                            wA = [(sAC, ln_act_last), (qsem, pair_done[j])]
                            wA.append((sAC, silu_m[j - 2] if j >= 2 else prev_silu[j]))
                            h1_pe[j] = P("tensor",
                                         lambda e, slot=slot, bank=bank_h1, ybuf=ybuf:
                                         mmA(e, ybuf, 0, slot, bank),
                                         waits=wA, inc=sPE)
                            wAg = [(sDV, mult_m[j - 2] if j >= 2 else prev_mult[j])]
                            hg_pe[j] = P("tensor",
                                         lambda e, slot=slot, bank=bank_hg, ybuf=ybuf:
                                         mmA(e, ybuf, 1, slot, bank),
                                         waits=wAg, inc=sPE)
                            pairA_end[j] = sPE.n
                            if conv:
                                if j % 2 == 1 and j + 4 < 16:
                                    jj = j + 3
                                    sl0 = jj % 4
                                    wv_ = [(sPE, pairA_end[j])]
                                    if cache_pair_done[jj - 4]:
                                        wv_.append((sW, cache_pair_done[jj - 4]))
                                    v2 = P(qeng, lambda e, ei=ei, jj=jj, sl0=sl0: e.dma_start(
                                        out=wf_rg.ap().rearrange("p (s m f) -> p s m f", s=4, m=2)[:, sl0:sl0 + 2],
                                        in_=wfs[ei, jj:jj + 2].rearrange("j m p f -> p j m f")),
                                        waits=wv_, inc=qsem, k=16)
                                    pair_done[jj] = pair_done[jj + 1] = v2
                                    if jj % 4 == 2:
                                        g0 = jj - 2
                                        cache_pair_done[g0] = cache_pair_done[g0 + 1] = \
                                            cache_pair_done[g0 + 2] = cache_pair_done[g0 + 3] = P(
                                            "sync", lambda e, ei=ei, g0=g0: e.dma_start(
                                                out=cwf[ei, g0:g0 + 4].rearrange("j m p f -> p j m f"),
                                                in_=wf_rg.ap().rearrange("p (s m f) -> p s m f", s=4, m=2)),
                                            waits=[(sG, pair_done[g0 + 3])], inc=sW, k=16)
                            elif j + 4 < 16:
                                s1 = slab_src(ei, j + 4, use32)
                                wv_ = [(sPE, pairA_end[j])]
                                pair_done[j + 4] = P(qeng, lambda e, s1=s1, j=j: e.dma_start(
                                    out=wf_rg.ap().rearrange("p (s m f) -> p s m f", s=4, m=2)[:, j % 4],
                                    in_=s1), waits=wv_, inc=qsem, k=16)

                            wS = [(sPE, h1_pe[j]),
                                  (sDV, mult_m[j - 2] if j >= 2 else prev_multw[j])]
                            silu_m[j] = P("scalar", lambda e, j=j, bank=bank_h1: e.activation(
                                s_rg[:, (j % 2) * TBLK:(j % 2 + 1) * TBLK], bank, AF.Silu),
                                waits=wS, inc=sAC)
                            wM = [(sAC, silu_m[j]), (sPE, hg_pe[j])]
                            if j == 0:
                                wM.append((sPE, passB_pe_end))
                            mult_m[j] = P("vector", lambda e, j=j, bank=bank_hg: e.tensor_tensor(
                                h_sb[:, j * TBLK:(j + 1) * TBLK], bank,
                                s_rg[:, (j % 2) * TBLK:(j % 2 + 1) * TBLK], op=ALU.mult),
                                waits=wM, inc=sDV)
                        prev_silu = [silu_m[14], silu_m[15]]
                        prev_mult = [mult_m[14], mult_m[15]]
                        prev_multw = [mult_m[14], mult_m[15]]
                        prev_pairA = [pairA_end[12], pairA_end[13],
                                      pairA_end[14], pairA_end[15]]

                        if 3 <= ei <= 6 and blk + 1 < NBLK:
                            ln_vy = emit_ln_ti_pre(blk + 1, ei - 3)
                        h_done = mult_m[15]
                        sttB = [0] * (NT * 4)
                        for ti in range(NT):
                            for qi in range(4):
                                idx = ti * 4 + qi
                                col = qi * 512
                                reg = (idx % 2) * 512
                                wB = [(sDV, h_done), (sG, vW2)]
                                if idx >= 2:
                                    wB.append((sDV, sttB[idx - 2]))
                                else:
                                    wB.append((sDV, sttB_tail[idx]))

                                def doB(e, ti=ti, col=col, reg=reg):
                                    last = None
                                    for j in range(16):
                                        last = e.matmul(
                                            ps_c[:, reg:reg + 512],
                                            h_sb[:, j * TBLK + ti * 128:
                                                 j * TBLK + (ti + 1) * 128],
                                            w2_sb[:, j * D + col:j * D + col + 512],
                                            start=(j == 0), stop=(j == 15))
                                    return last
                                vB = P("tensor", doB, waits=wB, inc=sPE)
                                wstt = [(sPE, vB)]
                                if ei == 0 and blk >= 0:
                                    wstt.append((sX, vxblk))
                                sttB[idx] = P("vector",
                                    lambda e, ti=ti, col=col, reg=reg, ei=ei, wbuf=wbuf:
                                    e.scalar_tensor_tensor(
                                        x_blk[:, ti * D + col:ti * D + col + 512],
                                        ps_c[:, reg:reg + 512],
                                        wbuf[:, ti * NE + ei:ti * NE + ei + 1],
                                        x_blk[:, ti * D + col:ti * D + col + 512],
                                        op0=ALU.mult, op1=ALU.add),
                                    waits=wstt, inc=sDV)
                        sttB_tail = [sttB[NT * 4 - 2], sttB[NT * 4 - 1]]
                        passB_pe_end = sPE.n
                        if conv:
                            cache_mark[ei] = sW.n
                        if not conv:
                            cache_pair_done = [0] * 16
                            w2_cache_mark = 0

                        if 3 <= ei <= 6 and blk + 1 < NBLK:
                            yT_done[1 - par], _ = emit_ln_ti(
                                blk + 1, ei - 3, yT_bufs[1 - par], w_alls[1 - par],
                                pe_blk_end[1 - par] if ei == 3 else 0, vy=ln_vy)

                    pe_blk_end[par] = sPE.n
                    conv_blk = first and blk == 0
                    store_sem = sX if conv_blk else sG
                    store_mark = P("scalar" if conv_blk else "gpsimd",
                        lambda e, blk=blk: e.dma_start(
                            out=dst[blk * TBLK:(blk + 1) * TBLK, :].rearrange(
                                "(t p) f -> p t f", p=128),
                            in_=x_blk.ap().rearrange("p (t f) -> p t f", t=NT)),
                        waits=[(sDV, sttB[NT * 4 - 1])], inc=store_sem, k=16)
                plan.flush()
            return store_mark

        def emit_attn():
            with ExitStack() as acm:
                def sbc(name, shape, dt):
                    return acm.enter_context(nc.sbuf_tensor(name, shape, dt))
                sq_scr = sbc("sq_scr", [128, D], BF16)
                kv_bd = sbc("kv_bd", [128, 2048], BF16)
                ksum_bd = sbc("ksum_bd", [128, 16], BF16)
                r_t = sbc("r_t", [128, 1024], F32)
                m_t = sbc("m_t", [128, 1024], F32)
                xa_rg = sbc("xa_rg", [128, 2 * D], F32)
                zb_rg = sbc("zb_rg", [128, D], BF16)
                zT_rg = sbc("zT_rg", [128, 2 * D], BF16)
                bvrow_b = sbc("bvrow_b", [1, D], BF16)
                borow_b = sbc("borow_b", [1, D], BF16)
                brow_b = sbc("brow_b", [1, 2048], BF16)
                wm_b = sbc("wm_b", [128, 64], BF16)
                bT_b = sbc("bT_b", [128, 16], BF16)

                def emit_ln_pre(sl, wx, war_pe):
                    vnb = emit_ln_stats(xa_rg[:, sl * D:(sl + 1) * D], sl * 8, wx,
                                        sq_scr[:])
                    wz = [(sDV, vnb)]
                    if war_pe:
                        wz.append((sPE, war_pe))
                    vz = P("scalar", lambda e, sl=sl: e.activation(
                        zb_rg[:], xa_rg[:, sl * D:(sl + 1) * D],
                        AF.Identity, bias=st[:, sl * 8 + 6:sl * 8 + 7],
                        scale=st[:, sl * 8 + 5:sl * 8 + 6]), waits=wz, inc=sAC)
                    return vz, sAC.n

                def emit_z_trans(sl, vz, war_dv, trans_ps, halfbank=False):
                    psab = trans_ps.ap().bitcast(BF16)
                    last_ev = war_dv
                    for hg in range(2):
                        off = 0 if halfbank else hg * 1024
                        wt = [(sAC, vz)]
                        if last_ev:
                            wt.append((sDV, last_ev))

                        def dot(e, hg=hg, off=off, psab=psab):
                            last = None
                            for i in range(8):
                                cc = hg * 8 + i
                                last = e.transpose(
                                    psab[:, off + i * 128:off + (i + 1) * 128],
                                    zb_rg[:, cc * 128:(cc + 1) * 128],
                                    ident_b[:])
                            return last
                        vt = P("tensor", dot, waits=wt, inc=sPE)
                        last_ev = P("vector", lambda e, hg=hg, off=off, sl=sl, psab=psab:
                            e.tensor_copy(
                                zT_rg[:, sl * D + hg * 1024:sl * D + (hg + 1) * 1024],
                                psab[:, off:off + 1024]),
                            waits=[(sPE, vt)], inc=sDV)
                    return last_ev, sPE.n

                def emit_compose(wm_d, bT_d, wrows, brow_off, wdst_sb, wh_rg,
                                 pre_loads=()):
                    vm = P("gpsimd", lambda e: e.dma_start(out=wm_b[:], in_=wm_d), inc=sG, k=16)
                    vbb = P("gpsimd", lambda e: e.dma_start(out=bT_b[:], in_=bT_d), inc=sG, k=16)
                    quad_done = [0] * 8
                    quad_pe = [0] * 8
                    first_extra = True
                    comp_pe = {}
                    comp_dv = {}

                    def load_quad(q):
                        wv_ = [(sG, vm)]
                        if q >= 2:
                            wv_.append((sPE, quad_pe[q - 2]))
                        quad_done[q] = P("gpsimd", lambda e, q=q: e.dma_start(
                            out=wh_rg.ap().rearrange("p (s h f) -> p s h f", s=2, h=4)[:, q % 2],
                            in_=wrows[4 * q:4 * q + 4].rearrange("h p f -> p h f")),
                            waits=wv_, inc=sG, k=16)

                    load_quad(0)
                    load_quad(1)
                    drip = list(pre_loads)
                    for h in range(16):
                        if h % 4 == 0 and drip:
                            drip.pop(0)()
                        q = h // 4
                        base = (q % 2) * 4 * D + (h % 4) * D
                        wc = [(sG, quad_done[q]), (sG, vbb)]
                        if h >= 1:
                            wc.append((sDV, comp_dv[h - 1]))

                        def doc(e, base=base):
                            last = None
                            for c in range(CH):
                                last = e.matmul(ps_b[:, c * 64:(c + 1) * 64],
                                                wh_rg[:, base + c * 128:
                                                      base + (c + 1) * 128],
                                                wm_b[:], start=True, stop=True)
                            return last
                        vpc = P("tensor", doc, waits=wc, inc=sPE)
                        vpb = P("tensor", lambda e, h=h: e.matmul(
                            ps_a[0:1, h * 64:(h + 1) * 64], bT_b[:, h:h + 1], wm_b[:],
                            start=True, stop=True), inc=sPE)
                        quad_pe[q] = sPE.n
                        if h % 4 == 3 and q + 2 < 4:
                            load_quad(q + 2)
                        ev = P("vector", lambda e, h=h: e.tensor_copy(
                            wdst_sb.ap().rearrange("p (c f) -> p c f", c=CH)
                            [:, :, h * 64:(h + 1) * 64],
                            ps_b.ap().rearrange("p (c f) -> p c f", c=16)),
                            waits=[(sPE, vpc)], inc=sDV)
                        evb = P("vector", lambda e, h=h, brow_off=brow_off: e.tensor_copy(
                            brow_b[0:1, brow_off + h * 64:brow_off + (h + 1) * 64],
                            ps_a[0:1, h * 64:(h + 1) * 64]),
                            waits=[(sPE, vpb)], inc=sDV)
                        comp_pe[h] = sPE.n
                        comp_dv[h] = max(ev, evb)
                    for fn in drip:
                        fn()
                    return sPE.n, sDV.n

                with ExitStack() as a1:
                    def sb1(name, shape, dt):
                        return a1.enter_context(nc.sbuf_tensor(name, shape, dt))
                    wkm_sb = sb1("wkm_sb", [128, CH * 1024], BF16)
                    wvT_b = sb1("wvT_b", [128, CH * D], BF16)
                    wh_rg = sb1("wh_rg", [128, 8 * D], BF16)
                    km_sb = sb1("km_sb", [128, 1024], BF16)
                    v_sb = sb1("v_sb", [128, 2048], BF16)
                    kv_acc = sb1("kv_acc", [128, 1024], F32)
                    ks_acc = sb1("ks_acc", [128, 8], F32)

                    plan.barrier()
                    P("vector", lambda e: e.memset(kv_acc[:], 0.0), inc=sDV)
                    P("vector", lambda e: e.memset(ks_acc[:], 0.0), inc=sDV)
                    ld = {}

                    def _ldbv():
                        ld["vbv"] = P("gpsimd", lambda e: e.dma_start(
                            out=bvrow_b[:], in_=bv_d), inc=sG, k=16)

                    def _ldwv(cc):
                        def f():
                            ld["vwv"] = P("gpsimd", lambda e, cc=cc: e.dma_start(
                                out=wvT_b.ap().rearrange("p (c f) -> p c f", c=CH)
                                [:, 4 * cc:4 * cc + 4],
                                in_=cwvT[4 * cc:4 * cc + 4].rearrange("c p f -> p c f")),
                                inc=sG, k=16)
                        return f
                    cpe, cdv = emit_compose(wmk_d, bkT_d, cwk, 0, wkm_sb, wh_rg,
                                            pre_loads=[_ldbv] + [_ldwv(c) for c in range(4)])
                    vbv = ld["vbv"]
                    vwv = ld["vwv"]
                    if ATTN_SUB <= 1:
                        plan.flush()
                        return

                    km_ev = 0
                    v_ev_last = 0
                    v_ev = [0, 0, 0, 0]
                    kv_pe = 0
                    kv_add_dv = 0
                    zt_prev = 0
                    xa_free = {0: (0, 0), 1: (0, 0)}

                    def a1_pre(ti):
                        sl = ti % 2
                        wld = [(sAC, xa_free[sl][0]), (sDV, xa_free[sl][1])]
                        vx = P("scalar", lambda e, ti=ti, sl=sl: e.dma_start(
                            out=xa_rg[:, sl * D:(sl + 1) * D],
                            in_=xA[ti * 128:(ti + 1) * 128, :]), waits=wld, inc=sX, k=16)
                        vz, zac = emit_ln_pre(sl, [(sX, vx)], 0)
                        xa_free[sl] = (zac, sDV.n)
                        return vz

                    vz_next, _zac = None, 0
                    vz_next = a1_pre(0)
                    zt_dv, zt_pe = emit_z_trans(0, vz_next, cdv, ps_b, halfbank=True)
                    zt_prev = zt_dv
                    zt_cur = zt_dv
                    for ti in range(16):
                        sl = ti % 2
                        wkm_ = [(sDV, zt_cur)]
                        if km_ev:
                            wkm_.append((sDV, km_ev))
                        if ti + 1 < 16:
                            vz_next = a1_pre(ti + 1)

                        def dokm(e, sl=sl):
                            last = None
                            for nh in range(2):
                                col = nh * 512
                                for c in range(CH):
                                    last = e.matmul(ps_a[:, col:col + 512],
                                                    zT_rg[:, sl * D + c * 128:
                                                          sl * D + (c + 1) * 128],
                                                    wkm_sb[:, c * 1024 + col:
                                                           c * 1024 + col + 512],
                                                    start=(c == 0), stop=False)
                                last = e.matmul(ps_a[:, col:col + 512], ones_b[:],
                                                brow_b[0:1, col:col + 512],
                                                start=False, stop=True)
                            return last
                        vkm = P("tensor", dokm, waits=wkm_, inc=sPE)
                        vb_ = P("vector", lambda e: e.tensor_scalar(
                            m_t[:], ps_a[:], 0.0, None, op0=ALU.min),
                            waits=[(sPE, vkm)], inc=sDV)
                        vc = P("scalar", lambda e: e.activation(m_t[:], m_t[:], AF.Exp),
                               waits=[(sDV, vb_)], inc=sAC)
                        wke = [(sAC, vc)]
                        if ti >= 1:
                            wke.append((sPE, kv_pe))
                        km_ev = P("vector", lambda e: e.scalar_tensor_tensor(
                            km_sb[:], ps_a[:], 0.0, m_t[:], op0=ALU.max, op1=ALU.add),
                            waits=wke, inc=sDV)

                        for n in range(4):
                            vreg = ps_c[:, 1024:1536] if n % 2 == 0 else ps_b[:, 512:1024]
                            wv_ = [(sDV, zt_cur), (sG, vwv), (sG, vbv)]
                            if v_ev[n - 2]:
                                wv_.append((sAC, v_ev[n - 2]))
                            if ti >= 1 and n == 0:
                                wv_.append((sPE, kv_pe))

                            def dov(e, n=n, sl=sl, vreg=vreg):
                                for c in range(CH):
                                    e.matmul(vreg,
                                             zT_rg[:, sl * D + c * 128:sl * D + (c + 1) * 128],
                                             wvT_b[:, c * D + n * 512:c * D + n * 512 + 512],
                                             start=(c == 0), stop=False)
                                return e.matmul(vreg, ones_b[:],
                                                bvrow_b[0:1, n * 512:(n + 1) * 512],
                                                start=False, stop=True)
                            vv = P("tensor", dov, waits=wv_, inc=sPE)
                            v_ev[n] = P("scalar", lambda e, n=n, vreg=vreg: e.activation(
                                v_sb[:, n * 512:(n + 1) * 512], vreg,
                                AF.Copy), waits=[(sPE, vv)], inc=sAC)
                            v_ev_last = v_ev[n]

                        if ti + 1 < 16:
                            zt_dv, zt_pe = emit_z_trans((ti + 1) % 2, vz_next,
                                                        zt_prev, ps_b, halfbank=True)
                            zt_prev = zt_dv
                            zt_nxt = zt_dv
                        wkv = [(sDV, km_ev)] + [(sAC, v_ev[n]) for n in range(4)]
                        if ti >= 1:
                            wkv.append((sDV, kv_add_dv))

                        def dokv(e, ti=ti):
                            last = None
                            for h in range(16):
                                po = (h % 2) * 64
                                fo = (h // 2) * 128
                                last = e.matmul(ps_c[po:po + 64, fo:fo + 128],
                                                km_sb[:, h * 64:(h + 1) * 64],
                                                v_sb[:, h * 128:(h + 1) * 128],
                                                start=True, stop=True)
                            for h in range(16):
                                po = (h % 2) * 64
                                last = e.matmul(
                                    ps_c[po:po + 64, 1536 + (h // 2):1537 + (h // 2)],
                                    km_sb[:, h * 64:(h + 1) * 64], onescol_b[:],
                                    start=True, stop=True)
                            return last
                        kv_pe = P("tensor", dokv, waits=wkv, inc=sPE)
                        P("vector", lambda e: e.tensor_tensor(
                            kv_acc[:], kv_acc[:], ps_c[:, 0:1024], op=ALU.add),
                            waits=[(sPE, kv_pe)], inc=sDV)
                        kv_add_dv = P("vector", lambda e: e.tensor_tensor(
                            ks_acc[:], ks_acc[:], ps_c[:, 1536:1544], op=ALU.add),
                            inc=sDV)
                        if ti + 1 < 16:
                            zt_cur = zt_nxt
                    P("vector", lambda e: e.memset(kv_bd[:], 0.0),
                      waits=[(sDV, kv_add_dv)], inc=sDV)
                    P("vector", lambda e: e.memset(ksum_bd[:], 0.0), inc=sDV)
                    for g in range(8):
                        P("vector", lambda e, g=g: e.tensor_copy(
                            kv_bd[0:64, g * 256:g * 256 + 128],
                            kv_acc[0:64, g * 128:(g + 1) * 128]), inc=sDV)
                        P("vector", lambda e, g=g: e.tensor_copy(
                            kv_bd[64:128, g * 256 + 128:g * 256 + 256],
                            kv_acc[64:128, g * 128:(g + 1) * 128]), inc=sDV)
                        P("vector", lambda e, g=g: e.tensor_copy(
                            ksum_bd[0:64, 2 * g:2 * g + 1],
                            ks_acc[0:64, g:g + 1]), inc=sDV)
                        P("vector", lambda e, g=g: e.tensor_copy(
                            ksum_bd[64:128, 2 * g + 1:2 * g + 2],
                            ks_acc[64:128, g:g + 1]), inc=sDV)
                    plan.flush()
                if ATTN_SUB <= 2:
                    return

                a23 = acm.enter_context(ExitStack())
                qmT_sb = a23.enter_context(nc.sbuf_tensor("qmT_sb", [128, 8 * 2048], BF16))
                rn_sb = a23.enter_context(nc.sbuf_tensor("rn_sb", [128, 256], F32))
                with ExitStack() as a2:
                    def sb2(name, shape, dt):
                        return a2.enter_context(nc.sbuf_tensor(name, shape, dt))
                    wqm_sb = sb2("wqm_sb", [128, CH * 1024], BF16)
                    wh_rg2 = sb2("wh_rg2", [128, 8 * D], BF16)
                    qm_tm = sb2("qm_tm", [128, 1024], BF16)

                    plan.barrier()
                    cpe2, cdv2 = emit_compose(wmq_d, bqT_d, cwq, 1024, wqm_sb, wh_rg2)

                    qm_ev = 0
                    qmT_ev = cdv2
                    qt_pe = 0
                    zt_prev2 = 0
                    xa_free = {0: (0, 0), 1: (0, 0)}

                    def a2_pre(ti):
                        sl = ti % 2
                        wld = [(sAC, xa_free[sl][0]), (sDV, xa_free[sl][1])]
                        vx = P("scalar", lambda e, ti=ti, sl=sl: e.dma_start(
                            out=xa_rg[:, sl * D:(sl + 1) * D],
                            in_=xA[ti * 128:(ti + 1) * 128, :]), waits=wld, inc=sX, k=16)
                        vz, zac = emit_ln_pre(sl, [(sX, vx)], 0)
                        xa_free[sl] = (zac, sDV.n)
                        return vz

                    vz_next = a2_pre(0)
                    zt_cur, _ = emit_z_trans(0, vz_next, cdv2, ps_c)
                    zt_prev2 = zt_cur
                    for ti in range(16):
                        sl = ti % 2
                        wqm_ = [(sDV, zt_cur)]
                        if qm_ev:
                            wqm_.append((sDV, qm_ev))
                        if ti + 1 < 16:
                            vz_next = a2_pre(ti + 1)

                        def doqm(e, sl=sl):
                            last = None
                            for nh in range(2):
                                col = nh * 512
                                for c in range(CH):
                                    last = e.matmul(ps_a[:, col:col + 512],
                                                    zT_rg[:, sl * D + c * 128:
                                                          sl * D + (c + 1) * 128],
                                                    wqm_sb[:, c * 1024 + col:
                                                           c * 1024 + col + 512],
                                                    start=(c == 0), stop=False)
                                last = e.matmul(ps_a[:, col:col + 512], ones_b[:],
                                                brow_b[0:1, 1024 + col:1024 + col + 512],
                                                start=False, stop=True)
                            return last
                        vqm = P("tensor", doqm, waits=wqm_, inc=sPE)
                        vb_ = P("vector", lambda e: e.tensor_scalar(
                            m_t[:], ps_a[:], 0.0, None, op0=ALU.min),
                            waits=[(sPE, vqm)], inc=sDV)
                        vc = P("scalar", lambda e: e.activation(m_t[:], m_t[:], AF.Exp),
                               waits=[(sDV, vb_)], inc=sAC)
                        wqe = [(sAC, vc)]
                        if ti >= 1:
                            wqe.append((sPE, qt_pe))
                        qm_ev = P("vector", lambda e: e.scalar_tensor_tensor(
                            qm_tm[:], ps_a[:], 0.0, m_t[:], op0=ALU.max, op1=ALU.add),
                            waits=wqe, inc=sDV)

                        if ti + 1 < 16:
                            zt_nxt, _ = emit_z_trans((ti + 1) % 2, vz_next,
                                                     zt_prev2, ps_c)
                            zt_prev2 = zt_nxt
                        psbb = ps_b.ap().bitcast(BF16)

                        def doqt(e, psbb=psbb):
                            last = None
                            for g in range(8):
                                last = e.transpose(psbb[:, g * 128:(g + 1) * 128],
                                                   qm_tm[:, g * 128:(g + 1) * 128],
                                                   ident_b[:])
                            return last
                        wqt = [(sDV, qm_ev)]
                        if qmT_ev:
                            wqt.append((sDV, qmT_ev))
                        vqt = P("tensor", doqt, waits=wqt, inc=sPE)
                        qt_pe = vqt
                        qmT_ev = P("vector", lambda e, ti=ti, psbb=psbb: e.tensor_copy(
                            qmT_sb.ap().rearrange("p (g t) -> p g t", g=8)
                            [:, :, ti * 128:(ti + 1) * 128],
                            psbb[:, 0:1024].rearrange("p (g t) -> p g t", g=8)),
                            waits=[(sPE, vqt)], inc=sDV)
                        if ti + 1 < 16:
                            zt_cur = zt_nxt
                    plan.flush()
                if ATTN_SUB <= 3:
                    return

                with ExitStack() as b3:
                    def sb3(name, shape, dt):
                        return b3.enter_context(nc.sbuf_tensor(name, shape, dt))
                    woT_b = sb3("woT_b", [128, CH * D], BF16)
                    dg_sb = sb3("dg_sb", [128, 2 * 2048], BF16)
                    at_b = sb3("at_b", [128, 2048], BF16)
                    atT_b = sb3("atT_b", [128, 2048], BF16)
                    xo = sb3("xo", [128, 2 * D], F32)

                    plan.barrier()
                    vwo = P("gpsimd", lambda e: e.dma_start(
                        out=woT_b.ap().rearrange("p (c f) -> p c f", c=CH),
                        in_=cwoT.rearrange("c p f -> p c f")), inc=sG, k=16)
                    vbob = 0
                    vbo = P("gpsimd", lambda e: e.dma_start(out=borow_b[:], in_=bo_d),
                            inc=sG, k=16)

                    nrm_ev = {0: 0, 1: 0}
                    for ti in range(16):
                        reg = (ti % 2) * 512

                        def donrm_pre(e, ti=ti, reg=reg):
                            last = None
                            for g in range(8):
                                last = e.matmul(
                                    ps_b[:, reg + 2 * g:reg + 2 * g + 2],
                                    qmT_sb[:, g * 2048 + ti * 128:
                                           g * 2048 + (ti + 1) * 128],
                                    ksum_bd[:, 2 * g:2 * g + 2], start=True, stop=True)
                            return last
                        wn = []
                        if nrm_ev[ti % 2]:
                            wn.append((sDV, nrm_ev[ti % 2]))
                        vn = P("tensor", donrm_pre, waits=wn, inc=sPE)
                        P("vector", lambda e, ti=ti, reg=reg: e.tensor_scalar(
                            rn_sb[:, ti * 16:(ti + 1) * 16], ps_b[:, reg:reg + 16],
                            AT_EPS, None, op0=ALU.add), waits=[(sPE, vn)], inc=sDV)
                        nrm_ev[ti % 2] = P("vector", lambda e, ti=ti: e.reciprocal(
                            rn_sb[:, ti * 16:(ti + 1) * 16],
                            rn_sb[:, ti * 16:(ti + 1) * 16]), inc=sDV)
                    rn_done = sDV.n

                    at_ac = 0
                    atT_dv = 0
                    vat_prev = 0
                    dg_mark = {0: 0, 1: 0}
                    o_pe_last = 0
                    o_half_dv = {0: 0, 1: 0}
                    o_dv_sl = {0: 0, 1: 0}
                    st_sl = {0: 0, 1: 0}

                    def donum(e, ti):
                        last = None
                        for g in range(8):
                            last = e.matmul(
                                ps_c[:, g * 256:(g + 1) * 256],
                                qmT_sb[:, g * 2048 + ti * 128:
                                       g * 2048 + (ti + 1) * 128],
                                kv_bd[:, g * 256:(g + 1) * 256],
                                start=True, stop=True)
                        return last

                    vnum = P("tensor", lambda e: donum(e, 0),
                             waits=[(sDV, rn_done)], inc=sPE)
                    for ti in range(16):
                        sl = ti % 2
                        wld = [(sDV, o_dv_sl[sl])]
                        vx = P("scalar", lambda e, ti=ti, sl=sl: e.dma_start(
                            out=xa_rg[:, sl * D:(sl + 1) * D],
                            in_=xA[ti * 128:(ti + 1) * 128, :]), waits=wld, inc=sX, k=16)

                        wat = [(sPE, vnum)]
                        if vat_prev:
                            wat.append((sPE, vat_prev))
                        at_ac = P("scalar", lambda e: e.activation(
                            at_b[:], ps_c[:], AF.Identity), waits=wat, inc=sAC)
                        if ti == 0:
                            for tb in (0, 1):
                                wdg = [(sDV, rn_done)]
                                for h in range(16):
                                    P("scalar", lambda e, tb=tb, h=h: e.activation(
                                        dg_sb[:, tb * 2048 + h * 128:tb * 2048 + (h + 1) * 128],
                                        ident_b[:], AF.Identity,
                                        scale=rn_sb[:, tb * 16 + h:tb * 16 + h + 1]),
                                        waits=(wdg if h == 0 else []), inc=sAC)
                                dg_mark[tb] = sAC.n
                        elif ti + 1 < 16:
                            tb = (ti + 1) % 2
                            wdg = [(sDV, rn_done), (sPE, vat_prev)]
                            for h in range(16):
                                P("scalar", lambda e, ti=ti, tb=tb, h=h: e.activation(
                                    dg_sb[:, tb * 2048 + h * 128:tb * 2048 + (h + 1) * 128],
                                    ident_b[:], AF.Identity,
                                    scale=rn_sb[:, (ti + 1) * 16 + h:(ti + 1) * 16 + h + 1]),
                                    waits=(wdg if h == 0 else []), inc=sAC)
                            dg_mark[tb] = sAC.n
                        dg_dv = dg_mark[ti % 2]
                        dgo = (ti % 2) * 2048

                        vat_h = [0, 0]
                        atT_half = [0, 0]
                        for rnd in range(2):
                            def doatt(e, rnd=rnd, dgo=dgo):
                                last = None
                                for cc in range(8):
                                    b = rnd * 8 + cc
                                    last = e.matmul(
                                        ps_a[:, cc * 128:(cc + 1) * 128],
                                        at_b[:, b * 128:(b + 1) * 128],
                                        dg_sb[:, dgo + b * 128:dgo + (b + 1) * 128],
                                        start=True, stop=True)
                                return last
                            watt = [(sAC, at_ac), (sAC, dg_dv)]
                            if atT_dv:
                                watt.append((sDV, atT_dv))
                            if rnd == 1:
                                watt = [(sDV, atT_half[0])]
                            vat_h[rnd] = P("tensor", doatt, waits=watt, inc=sPE)
                            atT_half[rnd] = P("vector", lambda e, rnd=rnd: e.tensor_copy(
                                atT_b[:, rnd * 1024:(rnd + 1) * 1024], ps_a[:]),
                                waits=[(sPE, vat_h[rnd])], inc=sDV)
                        vat = vat_h[1]
                        vat_prev = vat
                        atT_dv = atT_half[1]

                        if ti + 1 < 16:
                            vnum = P("tensor", lambda e, ti=ti: donum(e, ti + 1),
                                     waits=[(sAC, at_ac)], inc=sPE)

                        for n in range(4):
                            half = n % 2
                            wo_ = [(sDV, atT_dv), (sG, vwo), (sG, vbo),
                                   (sDV, o_half_dv[half] or rn_done)]

                            def doo(e, n=n, half=half):
                                for c in range(CH):
                                    e.matmul(ps_b[:, half * 512:half * 512 + 512],
                                             atT_b[:, c * 128:(c + 1) * 128],
                                             woT_b[:, c * D + n * 512:c * D + n * 512 + 512],
                                             start=(c == 0), stop=False)
                                return e.matmul(ps_b[:, half * 512:half * 512 + 512],
                                                ones_b[:],
                                                borow_b[0:1, n * 512:(n + 1) * 512],
                                                start=False, stop=True)
                            vo = P("tensor", doo, waits=wo_, inc=sPE)
                            wvo_ = [(sPE, vo), (sX, vx)]
                            if st_sl[sl]:
                                wvo_.append((sW, st_sl[sl]))
                            o_half_dv[half] = P("vector",
                                lambda e, n=n, half=half, sl=sl: e.tensor_tensor(
                                    xo[:, sl * D + n * 512:sl * D + (n + 1) * 512],
                                    ps_b[:, half * 512:half * 512 + 512],
                                    xa_rg[:, sl * D + n * 512:sl * D + (n + 1) * 512],
                                    op=ALU.add), waits=wvo_, inc=sDV)
                        o_pe_last = sPE.n
                        o_dv_sl[sl] = sDV.n
                        st_sl[sl] = P("sync", lambda e, ti=ti, sl=sl: e.dma_start(
                            out=xB[ti * 128:(ti + 1) * 128, :],
                            in_=xo[:, sl * D:(sl + 1) * D]),
                            waits=[(sDV, o_dv_sl[sl])], inc=sW, k=16)
                    plan.flush()

        emit_ffn(0, hs, xA)
        if phases >= 2:
            emit_attn()
        if phases >= 3:
            final_store = emit_ffn(1, xB, out_d)
        P("sync", lambda e: e.nop(),
          waits=[(sX, sX.n), (sW, sW.n), (sG, sG.n), (sPE, sPE.n),
                 (sDV, sDV.n), (sAC, sAC.n)])
        plan.flush()
    return nc


def _relayout(inputs):
    import ml_dtypes

    def g(k):
        return np.ascontiguousarray(np.asarray(inputs[k], dtype=np.float32))

    hs_all = g("L0_attn_hidden_states")
    w1_all = np.concatenate([g("L0_ffn_Ws1")[None], g("L0_ffn_W1")], 0)
    wg_all = np.concatenate([g("L0_ffn_Wsg")[None], g("L0_ffn_Wg")], 0)
    w2_all = np.concatenate([g("L0_ffn_Ws2")[None], g("L0_ffn_W2")], 0)
    w1s = w1_all.reshape(NE, 16, 128, 16, 128).transpose(0, 3, 2, 1, 4).reshape(
        NE, 16, 128, D)
    wgs = wg_all.reshape(NE, 16, 128, 16, 128).transpose(0, 3, 2, 1, 4).reshape(
        NE, 16, 128, D)
    wfs = np.ascontiguousarray(np.stack([w1s, wgs], axis=2))
    w2s = np.ascontiguousarray(w2_all.reshape(NE, 16, 128, D))
    wq = g("L0_attn_wq")
    wk = g("L0_attn_wk")
    wv = g("L0_attn_wv")
    wo = g("L0_attn_wo")
    com = dict(
        wfs=wfs, w2s=w2s,
        wq_r=np.ascontiguousarray(wq.reshape(16, 128, D)),
        wk_r=np.ascontiguousarray(wk.reshape(16, 128, D)),
        wvT_d=np.ascontiguousarray(wv.T).reshape(16, 128, D),
        woT_d=np.ascontiguousarray(wo.T).reshape(16, 128, D),
        wmq_d=np.ascontiguousarray(g("L0_attn_wmap_q").T),
        wmk_d=np.ascontiguousarray(g("L0_attn_wmap_k").T),
        bqT_d=np.ascontiguousarray(g("L0_attn_bq").reshape(16, 128).T),
        bkT_d=np.ascontiguousarray(g("L0_attn_bk").reshape(16, 128).T),
        bv_d=g("L0_attn_bv").reshape(1, D),
        bo_d=g("L0_attn_bo").reshape(1, D),
        gw_d=np.ascontiguousarray(
            g("L0_ffn_gate_w").reshape(16, 128, 7).transpose(1, 0, 2)).reshape(128, 112),
        gb_d=g("L0_ffn_gate_b").reshape(1, 7),
        be_d=g("L0_ffn_biases").reshape(1, 7),
        idf_d=np.eye(128, dtype=np.float32),
        idb_d=np.eye(128, dtype=np.float32).astype(ml_dtypes.bfloat16),
    )
    return hs_all, com


def kernel(**inputs):
    hs_all, com = _relayout(inputs)
    nc = build()
    in_maps = [dict(com, hs=np.ascontiguousarray(hs_all[i])) for i in range(8)]
    res = run_bass_kernel_spmd(nc, in_maps, core_ids=list(range(8)))
    out = np.stack([np.asarray(res.results[i]["out"]) for i in range(8)], 0)
    return out.astype(np.float32)


if __name__ == "__main__":
    nc = build()
    n = sum(1 for _ in nc.all_instructions())
    print("built ok, instructions:", n)
